# version 08
# baseline (speedup 1.0000x reference)
import numpy as np
import ml_dtypes
import concourse.bass as bass
import concourse.mybir as mybir
from concourse.bass_utils import run_bass_kernel_spmd

F32 = mybir.dt.float32
BF16 = mybir.dt.bfloat16
AX = mybir.AxisListType.X
AF = mybir.ActivationFunctionType
ALU = mybir.AluOpType
BF = ml_dtypes.bfloat16

D = 1024
T = 2048
NFC = 8
NH = 16
HD = 64
TC = 512
NTC = T // TC
KB = 1024
AQ_ITERS = None
AQ_PLAIN = False
AQ_EPI = 99

SB_CEIL = 212864
OFF_AROW = 24576
OFF_BROW = 32768
OFF_RTMP = 40960
OFF_AB = 45056
OFF_CONST = 49152
OFF_SLOT0 = 51200
OFF_SLOT1 = 68096
OFF_T = 84992
OFF_BIG = 147328

T_SQ = OFF_T + 0
T_XN = OFF_T + 8192
T_H1 = OFF_T + 16384
T_H2 = OFF_T + 24576
T_Q = OFF_T + 32768
T_AT = OFF_T + 40960
T_KV = OFF_T + 49152
T_KVB = OFF_T + 51200
T_G = OFF_T + 52224
T_VT = OFF_T + 54272


class S:
    def __init__(self, nc, name):
        self.h = nc.alloc_semaphore(name)
        self.n = 0

    def tick(self, ins, k=1):
        ins.then_inc(self.h, k)
        self.n += k
        return self.n


class Build:
    def __init__(self):
        nc = bass.Bass("TRN2", target_bir_lowering=False)
        self.nc = nc
        self.input_names = []
        self.qS = S(nc, "qS")
        self.qG = S(nc, "qG")
        self.qC = S(nc, "qC")
        self.qX = [S(nc, "qX0"), S(nc, "qX1")]
        self.qW = [S(nc, "qW0"), S(nc, "qW1")]
        self.qV = [S(nc, "qV0"), S(nc, "qV1")]
        self.qA = S(nc, "qA")
        self.qT = [S(nc, "qT0"), S(nc, "qT1")]
        self.sPE = S(nc, "sPE")
        self.sVE = S(nc, "sVE")
        self.sSE = S(nc, "sSE")
        self.PB = [nc.alloc_psum_tensor(f"pb{i}", [128, 512], F32).ap() for i in range(8)]
        self.bank_free = [None] * 8
        self.ranges = []

    def din(self, name, shape, dtype=F32):
        ap = self.nc.dram_tensor(name, list(shape), dtype, kind="ExternalInput").ap()
        self.input_names.append(name)
        return ap

    def dout(self, name, shape, dtype=F32):
        return self.nc.dram_tensor(name, list(shape), dtype, kind="ExternalOutput").ap()

    def dscratch(self, name, shape, dtype=F32):
        return self.nc.dram_tensor(name, list(shape), dtype, kind="Internal").ap()

    def sba(self, name, shape, dtype, offset):
        esz = 2 if dtype == BF16 else 4
        nbytes = int(np.prod(shape[1:])) * esz
        assert offset >= 24576 and offset + nbytes <= SB_CEIL, (name, offset, nbytes)
        self.ranges.append((name, offset, offset + nbytes))
        return self.nc.alloc_sbuf_tensor_at(name, shape, dtype, offset=offset).ap()

    def pe_wait_bank(self, i):
        if self.bank_free[i]:
            for sem, val in self.bank_free[i]:
                self.nc.tensor.wait_ge(sem.h, val)
            self.bank_free[i] = None

    def evac_marks_bank(self, ins, i, sem):
        val = sem.tick(ins)
        self.bank_free[i] = [(sem, val)]
        return val

    def add_bank_reader(self, i, sem, val):
        if not self.bank_free[i]:
            self.bank_free[i] = []
        self.bank_free[i].append((sem, val))

    def barrier(self):
        self.nc.all_engine_barrier()
        self.bank_free = [None] * 8

    def audit(self):
        j = self.nc.to_json()
        mine = set()
        for nm, a0, a1 in self.ranges:
            mine.add(nm)
        bad = []
        for al in j["functions"][0]["allocations"]:
            for ml in al.get("memorylocations", []):
                if ml.get("type") != "SB":
                    continue
                a0 = ml["addr"]
                a1 = a0 + ml["dims"][1]
                nm = ml["name"]
                base = nm.split("_set")[0]
                if not any(base.startswith(r) for r in mine) and a1 > 24576:
                    bad.append((nm, a0, a1))
        assert not bad, f"auto allocs entered arena: {bad}"


def emit_l0(b: Build, x_in, debug=None):
    nc = b.nc

    def dbg(name, src_ap):
        if debug is not None and name in debug:
            v = b.qS.tick(nc.sync.dma_start(out=debug[name], in_=src_ap), 16)
            nc.sync.wait_ge(b.qS.h, v)
            b.barrier()

    wqT = b.din("L0_wqT", (D, D))
    wkT = b.din("L0_wkT", (D, D))
    wvT = b.din("L0_wvT", (D, D))
    woT = b.din("L0_woT", (D, D))
    w1 = b.din("L0_w1", (D, D))
    wa1 = b.din("L0_wa1", (D, D))
    wa2 = b.din("L0_wa2", (D, D))
    fbias = b.din("L0_fbias", (128, 8))
    ones128 = b.din("C_ones128", (128, 1))
    ones128b = b.din("C_ones128b", (128, 1), BF16)
    one1 = b.din("C_one1", (1, 128))
    idn = b.din("C_idn", (128, 128))
    eps = b.din("C_eps", (128, 2))

    x_fm = b.sba("x_fm", [128, NFC, T], F32, OFF_BIG)
    WS = [
        b.sba("wslot0", [128, NFC, KB], BF16, OFF_SLOT0),
        b.sba("wslot1", [128, NFC, KB], BF16, OFF_SLOT1),
    ]
    x_tm = b.sba("x_tm", [128, 2, KB], F32, T_H1)
    sq = b.sba("sq", [128, NFC, TC], BF16, T_SQ)
    xn = b.sba("xn", [128, NFC, TC], BF16, T_XN)
    h1_bf = b.sba("h1_bf", [128, NFC, TC], BF16, T_H1)
    h2_bf = b.sba("h2_bf", [128, NFC, TC], BF16, T_H2)
    q_bf = b.sba("q_bf", [128, NFC, TC], BF16, T_Q)
    at_bf = b.sba("at_bf", [128, NFC, TC], BF16, T_AT)
    kt_bf = b.sba("kt_bf", [128, 2, KB], BF16, T_Q)
    vt_bf = b.sba("vt_bf", [128, 2, KB], BF16, T_AT)
    kv_acc = b.sba("kv_acc", [128, 512], F32, T_KV)
    kv_sb = b.sba("kv_sb", [128, 512], BF16, T_KVB)
    g_oc = b.sba("g_oc", [128, TC], F32, T_G)
    vtmp = b.sba("vtmp", [128, TC], F32, T_VT)
    a_ln = b.sba("a_ln", [1, T], F32, OFF_AROW)
    bneg_ln = b.sba("bneg_ln", [1, T], F32, OFF_BROW)
    rowtmp = b.sba("rowtmp", [1, 2, TC], F32, OFF_RTMP)
    ab_sb = b.sba("ab_sb", [128, 2, TC], F32, OFF_AB)
    ones_sb = b.sba("ones_sb", [128, 1], F32, OFF_CONST)
    onesb_sb = b.sba("onesb_sb", [128, 1], BF16, OFF_CONST + 32)
    one1_sb = b.sba("one1_sb", [1, 128], F32, OFF_CONST + 64)
    fb_sb = b.sba("fb_sb", [128, 8], F32, OFF_CONST + 576)
    eps_sb = b.sba("eps_sb", [128, 2], F32, OFF_CONST + 608)
    idn_sb = b.sba("idn_sb", [128, 128], F32, OFF_CONST + 640)

    for src, dst in (
        (ones128, ones_sb), (ones128b, onesb_sb), (one1, one1_sb),
        (idn, idn_sb), (fbias, fb_sb), (eps, eps_sb),
    ):
        b.qC.tick(nc.sync.dma_start(out=dst[:], in_=src), 16)
    const_loaded = b.qC.n

    def cast_load(slot, src, wait_pe=0):
        ins = nc.gpsimd.dma_start(
            out=WS[slot][:], in_=src.rearrange("(k p) o -> p k o", p=128)
        )
        if wait_pe:
            ins._wait_ge(b.sPE.h, wait_pe)
        return slot, b.qW[slot].tick(ins, 16)

    cast_load(0, wkT)
    cast_load(1, wvT)

    x_tm_free = [0, 0]
    for tt in range(16):
        buf = tt % 2
        ins = nc.sync.dma_start(out=x_tm[:, buf, :], in_=x_in[tt * 128:(tt + 1) * 128, :])
        if x_tm_free[buf]:
            ins._wait_ge(b.sPE.h, x_tm_free[buf])
        b.qX[buf].tick(ins, 16)
        bk = [(0, 1), (2, 3)][buf]
        nc.tensor.wait_ge(b.qX[buf].h, b.qX[buf].n)
        if tt == 0:
            nc.tensor.wait_ge(b.qC.h, b.qC.n)
        b.pe_wait_bank(bk[0])
        b.pe_wait_bank(bk[1])
        for fc in range(NFC):
            ins = nc.tensor.transpose(
                b.PB[bk[fc // 4]][:, (fc % 4) * 128:(fc % 4 + 1) * 128],
                x_tm[:, buf, fc * 128:(fc + 1) * 128],
                idn_sb[:],
            )
        x_tm_free[buf] = b.sPE.tick(ins)
        pe_done = b.sPE.n
        for j, bi in enumerate(bk):
            ins = nc.vector.tensor_copy(
                x_fm[:, j * 4:(j + 1) * 4, tt * 128:(tt + 1) * 128],
                b.PB[bi].rearrange("p (a c) -> p a c", c=128),
            )
            if j == 0:
                ins._wait_ge(b.sPE.h, pe_done)
            b.evac_marks_bank(ins, bi, b.sVE)
    b.barrier()
    dbg("x_fm0", x_fm[:])

    def emit_ln_rows(tc):
        ts = tc * TC
        xsl = x_fm[:, :, ts:ts + TC]
        ins = nc.scalar.activation(sq[:], xsl, AF.Square)
        sq_done = b.sSE.tick(ins)
        bA, bB = 4, 5
        b.pe_wait_bank(bA)
        b.pe_wait_bank(bB)
        for fc in range(NFC):
            ins = nc.tensor.matmul(
                b.PB[bA][0:1, :], ones_sb[:], xsl[:, fc, :], start=(fc == 0), stop=(fc == 7)
            )
        sum_done = b.sPE.tick(ins)
        nc.tensor.wait_ge(b.sSE.h, sq_done)
        for fc in range(NFC):
            ins = nc.tensor.matmul(
                b.PB[bB][0:1, :], onesb_sb[:], sq[:, fc, :], start=(fc == 0), stop=(fc == 7)
            )
        sq_sum_done = b.sPE.tick(ins)
        ins = nc.scalar.mul(rowtmp[0:1, 0, :], b.PB[bA][0:1, :], 1.0 / D)
        ins._wait_ge(b.sPE.h, sum_done)
        b.bank_free[bA] = [(b.sSE, b.sSE.tick(ins))]
        ins = nc.scalar.mul(rowtmp[0:1, 1, :], b.PB[bB][0:1, :], 1.0 / D)
        ins._wait_ge(b.sPE.h, sq_sum_done)
        b.bank_free[bB] = [(b.sSE, b.sSE.tick(ins))]
        ins = nc.scalar.activation(a_ln[0:1, ts:ts + TC], rowtmp[0:1, 0, :], AF.Square)
        se_done = b.sSE.tick(ins)
        ins = nc.vector.tensor_sub(rowtmp[0:1, 1, :], rowtmp[0:1, 1, :], a_ln[0:1, ts:ts + TC])
        ins._wait_ge(b.sSE.h, se_done)
        ve_done = b.sVE.tick(ins)
        ins = nc.scalar.activation(rowtmp[0:1, 1, :], rowtmp[0:1, 1, :], AF.Sqrt, bias=eps_sb[0:1, 0:1])
        ins._wait_ge(b.sVE.h, ve_done)
        se_done = b.sSE.tick(ins)
        ins = nc.vector.reciprocal(a_ln[0:1, ts:ts + TC], rowtmp[0:1, 1, :])
        ins._wait_ge(b.sSE.h, se_done)
        b.sVE.tick(ins)
        ins = nc.vector.tensor_mul(
            bneg_ln[0:1, ts:ts + TC], rowtmp[0:1, 0, :], a_ln[0:1, ts:ts + TC]
        )
        return b.sVE.tick(ins), sq_sum_done

    def emit_xn(tc, rows_done, sq_reader_pe=0, dst_sl=None):
        if dst_sl is None:
            dst_sl = lambda fc: xn[:, fc, :]
        ts = tc * TC
        bA, bB = 4, 5
        b.pe_wait_bank(bA)
        b.pe_wait_bank(bB)
        nc.tensor.wait_ge(b.sVE.h, rows_done)
        nc.tensor.matmul(b.PB[bA][:], one1_sb[:], a_ln[0:1, ts:ts + TC], start=True, stop=True)
        ins = nc.tensor.matmul(b.PB[bB][:], one1_sb[:], bneg_ln[0:1, ts:ts + TC], start=True, stop=True)
        bc_done = b.sPE.tick(ins)
        ins = nc.vector.tensor_copy(ab_sb[:, 0, :], b.PB[bA][:])
        ins._wait_ge(b.sPE.h, bc_done)
        b.evac_marks_bank(ins, bA, b.sVE)
        ins = nc.vector.tensor_copy(ab_sb[:, 1, :], b.PB[bB][:])
        b.evac_marks_bank(ins, bB, b.sVE)
        if sq_reader_pe:
            nc.vector.wait_ge(b.sPE.h, sq_reader_pe)
        for fc in range(NFC):
            nc.vector.tensor_mul(vtmp[:], x_fm[:, fc, ts:ts + TC], ab_sb[:, 0, :])
            ins = nc.vector.tensor_sub(dst_sl(fc), vtmp[:], ab_sb[:, 1, :])
        return b.sVE.tick(ins)

    def proj(wslot, rhs_src, w_rdy, rhs_sem, rhs_done, banks, evac, n_oc=NFC):
        nc.tensor.wait_ge(b.qW[wslot].h, b.qW[wslot].n)
        nc.tensor.wait_ge(rhs_sem.h, rhs_done)
        for oc in range(n_oc):
            bk = banks[oc % len(banks)]
            b.pe_wait_bank(bk)
            for kc in range(NFC):
                ins = nc.tensor.matmul(
                    b.PB[bk][:],
                    WS[wslot][:, kc, oc * 128:(oc + 1) * 128],
                    rhs_src(kc),
                    start=(kc == 0),
                    stop=(kc == 7),
                )
            mm_done = b.sPE.tick(ins)
            evac(oc, bk, mm_done)
        return b.sPE.n

    def ve_evac(dst):
        def f(oc, bk, mm_done):
            ins = nc.vector.tensor_copy(dst[:, oc, :], b.PB[bk][:])
            ins._wait_ge(b.sPE.h, mm_done)
            b.evac_marks_bank(ins, bk, b.sVE)
        return f

    for tc in range(NTC):
        rows_done, sq_pe = emit_ln_rows(tc)
        xn_done = emit_xn(tc, rows_done, sq_reader_pe=sq_pe)
        if tc == 0 and debug:
            b.barrier()
            dbg("a_ln", a_ln[:])
            dbg("xn0", xn[:])
        for j in range(4):
            tt = tc * 4 + j
            buf = j % 2
            nc.tensor.wait_ge(b.sVE.h, xn_done)
            if tt == 0:
                nc.tensor.wait_ge(b.qW[0].h, b.qW[0].n)
                nc.tensor.wait_ge(b.qW[1].h, b.qW[1].n)
            for half in range(2):
                b.pe_wait_bank(half)
                for kc in range(NFC):
                    ins = nc.tensor.matmul(
                        b.PB[half][:],
                        xn[:, kc, j * 128:(j + 1) * 128],
                        WS[0][:, kc, half * 512:(half + 1) * 512],
                        start=(kc == 0), stop=(kc == 7),
                    )
                b.sPE.tick(ins)
            for half in range(2):
                b.pe_wait_bank(2 + half)
                for kc in range(NFC):
                    ins = nc.tensor.matmul(
                        b.PB[2 + half][:],
                        xn[:, kc, j * 128:(j + 1) * 128],
                        WS[1][:, kc, half * 512:(half + 1) * 512],
                        start=(kc == 0), stop=(kc == 7),
                    )
                b.sPE.tick(ins)
            kvmm_done = b.sPE.n
            first = True
            for half in range(2):
                ins = nc.vector.tensor_copy(kt_bf[:, buf, half * 512:(half + 1) * 512], b.PB[half][:])
                if first:
                    ins._wait_ge(b.sPE.h, kvmm_done)
                    first = False
                b.evac_marks_bank(ins, half, b.sVE)
                ins = nc.vector.tensor_copy(vt_bf[:, buf, half * 512:(half + 1) * 512], b.PB[2 + half][:])
                b.evac_marks_bank(ins, 2 + half, b.sVE)
            ev_done = b.sVE.n
            b.pe_wait_bank(6)
            nc.tensor.wait_ge(b.sVE.h, ev_done)
            for h in range(NH):
                rb = (h % 2) * 64
                cb = (h // 2) * 64
                ins = nc.tensor.matmul(
                    b.PB[6][rb:rb + 64, cb:cb + 64],
                    kt_bf[:, buf, h * 64:(h + 1) * 64],
                    vt_bf[:, buf, h * 64:(h + 1) * 64],
                    start=True, stop=True,
                )
            kvp_done = b.sPE.tick(ins)
            if tt == 0:
                ins = nc.vector.tensor_copy(kv_acc[:], b.PB[6][:])
            else:
                ins = nc.vector.tensor_add(kv_acc[:], kv_acc[:], b.PB[6][:])
            ins._wait_ge(b.sPE.h, kvp_done)
            b.evac_marks_bank(ins, 6, b.sVE)
        b.barrier()
    ins = nc.vector.tensor_copy(kv_sb[:], kv_acc[:])
    b.sVE.tick(ins)
    b.barrier()
    dbg("kv", kv_acc[:])

    for tc in range(NTC):
        ts = tc * TC
        cast_load(0, wqT)
        cast_load(1, w1)
        xn_done = emit_xn(tc, b.sVE.n)
        xn_sl = lambda kc: xn[:, kc, :]
        q_done = proj(0, xn_sl, None, b.sVE, xn_done, (0, 1), ve_evac(q_bf))
        cast_load(0, wa1, wait_pe=q_done)
        nc.tensor.wait_ge(b.sVE.h, b.sVE.n)
        for hp in range(8):
            bk = 2 + hp % 2
            b.pe_wait_bank(bk)
            for sub in range(2):
                h = 2 * hp + sub
                rb = (h % 2) * 64
                ins = nc.tensor.matmul(
                    b.PB[bk][rb:rb + 64, :],
                    kv_sb[rb:rb + 64, (h // 2) * 64:(h // 2 + 1) * 64],
                    q_bf[rb:rb + 64, h // 2, :],
                    start=True, stop=True,
                )
            at_done = b.sPE.tick(ins)
            ins = nc.scalar.activation(at_bf[:, hp, :], b.PB[bk][:], AF.Copy, scale=0.125)
            ins._wait_ge(b.sPE.h, at_done)
            b.evac_marks_bank(ins, bk, b.sSE)
        at_ready = b.sSE.n
        h1_done = proj(1, xn_sl, None, b.sVE, xn_done, (0, 1), ve_evac(h1_bf))
        cast_load(1, wa2, wait_pe=h1_done)
        h2_done = proj(0, lambda kc: h1_bf[:, kc, :], None, b.sVE, b.sVE.n, (2, 3), ve_evac(h2_bf))
        cast_load(0, woT, wait_pe=h2_done)
        nc.tensor.wait_ge(b.qW[0].h, b.qW[0].n)
        nc.tensor.wait_ge(b.qW[1].h, b.qW[1].n)
        nc.tensor.wait_ge(b.sVE.h, b.sVE.n)
        nc.tensor.wait_ge(b.sSE.h, at_ready)
        for oc in range(NFC):
            bg = 4 + (oc % 2)
            bs = 6 + (oc % 2)
            b.pe_wait_bank(bg)
            for kc in range(NFC):
                ins = nc.tensor.matmul(
                    b.PB[bg][:], WS[1][:, kc, oc * 128:(oc + 1) * 128],
                    h2_bf[:, kc, :], start=(kc == 0), stop=(kc == 7),
                )
            g_mm = b.sPE.tick(ins)
            b.pe_wait_bank(bs)
            for kc in range(NFC):
                ins = nc.tensor.matmul(
                    b.PB[bs][:], WS[0][:, kc, oc * 128:(oc + 1) * 128],
                    at_bf[:, kc, :], start=(kc == 0), stop=(kc == 7),
                )
            s_mm = b.sPE.tick(ins)
            ins = nc.scalar.activation(g_oc[:], b.PB[bg][:], AF.Tanh, bias=fb_sb[:, oc:oc + 1])
            ins._wait_ge(b.sPE.h, g_mm)
            b.evac_marks_bank(ins, bg, b.sSE)
            ins = nc.scalar.activation(g_oc[:], g_oc[:], AF.Silu)
            silu_done = b.sSE.tick(ins)
            nc.vector.wait_ge(b.sSE.h, silu_done)
            ins = nc.vector.tensor_mul(vtmp[:], b.PB[bs][:], g_oc[:])
            ins._wait_ge(b.sPE.h, s_mm)
            b.evac_marks_bank(ins, bs, b.sVE)
            nc.vector.tensor_add(x_fm[:, oc, ts:ts + TC], x_fm[:, oc, ts:ts + TC], vtmp[:])
        b.barrier()
        if tc == 0:
            dbg("q0", q_bf[:])
            dbg("at0", at_bf[:])

    return dict(
        x_fm=x_fm, emit_ln_rows=emit_ln_rows, emit_xn=emit_xn, dbg=dbg,
        ones_sb=ones_sb, onesb_sb=onesb_sb, one1_sb=one1_sb, eps_sb=eps_sb,
        idn_sb=idn_sb, a_ln=a_ln, bneg_ln=bneg_ln, rowtmp=rowtmp, ab_sb=ab_sb,
        vtmp=vtmp,
    )



def l0_host_inputs(ins):
    c = np.ascontiguousarray
    return {
        "L0_wqT": c(ins["L0_attn_W_q"].T),
        "L0_wkT": c(ins["L0_attn_W_k"].T),
        "L0_wvT": c(ins["L0_attn_W_v"].T),
        "L0_woT": c(ins["L0_attn_W_o"].T),
        "L0_w1": c(ins["L0_ffn_weight"]),
        "L0_wa1": c(ins["L0_ffn_aux_weight1"]),
        "L0_wa2": c(ins["L0_ffn_aux_weight2"]),
        "L0_fbias": c(ins["L0_ffn_bias"].reshape(8, 128).T),
        "C_ones128": np.ones((128, 1), np.float32),
        "C_ones128b": np.ones((128, 1), BF),
        "C_one1": np.ones((1, 128), np.float32),
        "C_idn": np.eye(128, dtype=np.float32),
        "C_eps": np.ascontiguousarray(np.broadcast_to(np.array([[1e-5, 1e-6]], np.float32), (128, 2))),
    }


OFF_XN2 = 93184

def emit_l1_attn(b: Build, c, debug=None, s_d=None, x1_d=None, until='C'):
    nc = b.nc
    dbg = c["dbg"]
    x_fm = c["x_fm"]
    one1_sb = c["one1_sb"]
    ones_sb = c["ones_sb"]
    onesb_sb = c["onesb_sb"]
    eps_sb = c["eps_sb"]
    idn_sb = c["idn_sb"]

    wq = b.din("L1_wq", (D, 1040))
    wk = b.din("L1_wk", (D, D))
    wv = b.din("L1_wv", (D, D))
    wo = b.din("L1_wo", (D, D))
    bqq = b.din("L1_bqq", (128, 8))
    bqg = b.din("L1_bqg", (16, 1))
    bkk = b.din("L1_bkk", (128, 8))
    bvpad = b.din("L1_bvpad", (128, 16, 64), BF16)
    boc = b.din("L1_boc", (128, 8))
    eqn = b.din("L1_eqn", (2, 128), BF16)
    ekn = b.din("L1_ekn", (2, 128), BF16)
    mask2 = b.din("L1_mask2", (128, 2), BF16)
    one1b = b.din("C_one1b", (1, 128), BF16)
    ones65f = b.din("C_ones65f", (65, 64))
    ones65b = b.din("C_ones65b", (65, 64), BF16)
    ones2048b = b.din("C_ones2048b", (1, T), BF16)
    ones16b = b.din("C_ones16b", (128, 16), BF16)

    if x1_d is None:
        x1_d = b.dscratch("x1_d", (128, NFC, T))
    v_d = b.dscratch("v_d", (16, 128, 16, 65), BF16)
    mx_d = b.dscratch("mx_d", (256, 128))
    g_d = b.dscratch("g_d", (16, T))
    at_d = b.dscratch("at_d", (128, NFC, T), BF16)
    if s_d is None:
        s_d = b.dscratch("s_d", (128, NFC, T))

    xn2 = b.sba("xn2", [128, NFC, T], BF16, OFF_XN2)
    Qh = b.sba("Qh", [128, NFC, T], BF16, OFF_BIG)
    Kh = b.sba("Kh", [128, NFC, T], BF16, OFF_BIG + 32768)
    wsl_q = b.sba("wsl_q", [128, NFC, 1040], BF16, OFF_SLOT0)
    wsl_v = b.sba("wsl_v", [128, NFC, KB], BF16, OFF_SLOT0)
    wsl_1 = b.sba("wsl_1", [128, NFC, KB], BF16, OFF_SLOT1)
    sq2 = b.sba("sq2", [128, 2, TC], BF16, 131072)
    qpre = b.sba("qpre", [128, 2, TC], F32, 133120)
    bc2m = b.sba("bc2m", [128, TC], F32, 126976)
    vstage = b.sba("vstage", [128, 2, 16, 65], BF16, 131072)
    vslab = b.sba("vslab", [128, 2, 16, 65], BF16, 135232)
    PT = b.sba("PT", [128, 2048], BF16, 139392)
    mxneg = b.sba("mxneg", [128, 256], F32, 143488)
    atst = b.sba("atst", [64, TC], BF16, 144512)
    m4 = b.sba("m4", [128, 8], F32, 145536)
    sstage = b.sba("sstage", [128, 2, TC], F32, 126976)
    at_slab = b.sba("at_slab", [128, NFC, TC], BF16, 131072)
    G_sb = b.sba("G_sb", [16, T], F32, OFF_AROW)
    g64 = b.sba("g64", [65, T], F32, OFF_AROW)
    rs64 = b.sba("rs64", [65, TC], F32, OFF_BROW)
    rrow = b.sba("rrow", [2, 2, TC], F32, OFF_RTMP)
    rrowb = b.sba("rrowb", [2, TC], BF16, OFF_AB + 3072)
    mxT = b.sba("mxT", [128, 2, 128], F32, OFF_RTMP)
    Qx = b.sba("Qx", [65, T], BF16, 40960)
    Kx = b.sba("Kx", [65, T], BF16, 45056)
    ones65f_sb = b.sba("ones65f_sb", [65, 64], F32, 32768 + 2048)
    rs64b = b.sba("rs64b", [65, TC], BF16, 32768 + 2304)
    ones65b_sb = b.sba("ones65b_sb", [65, 64], BF16, 32768 + 3328)
    eqn_sb = b.sba("eqn_sb", [2, 128], BF16, OFF_AB)
    ekn_sb = b.sba("ekn_sb", [2, 128], BF16, OFF_AB + 512)
    one1b_sb = b.sba("one1b_sb", [1, 128], BF16, OFF_CONST + 1152)
    c["one1b_sb"] = one1b_sb
    ones16b_sb = b.sba("ones16b_sb", [128, 16], BF16, OFF_CONST + 1760)
    mask2_sb = b.sba("mask2_sb", [128, 2], BF16, OFF_CONST + 1408)
    bqq_sb = b.sba("bqq_sb", [128, 8], F32, OFF_CONST + 1440)
    bkk_sb = b.sba("bkk_sb", [128, 8], F32, OFF_CONST + 1472)
    boc_sb = b.sba("boc_sb", [128, 8], F32, OFF_CONST + 1504)
    bqg_sb = b.sba("bqg_sb", [16, 1], F32, OFF_CONST + 1536)
    bvp_sb = b.sba("bvp_sb", [128, 16, 64], BF16, OFF_AB + 1024)


    for tc in range(NTC):
        rows_done, sq_pe = c["emit_ln_rows"](tc)
        ts = tc * TC
        c["emit_xn"](tc, rows_done, sq_reader_pe=sq_pe,
                     dst_sl=lambda fc: xn2[:, fc, ts:ts + TC])
    b.barrier()
    dbg("xn2", xn2[:])
    spill = b.qS.tick(nc.sync.dma_start(out=x1_d, in_=x_fm[:]), 16)

    for src, dst in ((eqn, eqn_sb), (ekn, ekn_sb), (mask2, mask2_sb),
                     (bqq, bqq_sb), (bkk, bkk_sb), (boc, boc_sb),
                     (bqg, bqg_sb), (one1b, one1b_sb), (bvpad, bvp_sb),
                     (ones65f, ones65f_sb), (ones16b, ones16b_sb),
                     (ones65b, ones65b_sb)):
        b.qS.tick(nc.sync.dma_start(out=dst[:], in_=src), 16)
    aconst = b.qS.n
    b.qW[0].tick(nc.gpsimd.dma_start(
        out=wsl_q[:], in_=wq.rearrange("(k p) o -> p k o", p=128)), 16)
    b.qW[1].tick(nc.gpsimd.dma_start(
        out=wsl_1[:], in_=wk.rearrange("(k p) o -> p k o", p=128)), 16)

    def proj_rms(dst, wslot_ap, bcol, expander, spill_wait):
        pend = None

        def epilogue(st):
            oc, tc, bk, par, sq2_done = st
            ts = tc * TC
            b.pe_wait_bank(2)
            nc.tensor.wait_ge(b.sSE.h, sq2_done)
            ins = nc.tensor.matmul(b.PB[2][0:2, :], mask2_sb[:], sq2[:, par, :], start=True, stop=True)
            r_done = b.sPE.tick(ins)
            nc.scalar.wait_ge(b.sPE.h, r_done)
            ins = nc.scalar.activation(rrow[:, par, :], b.PB[2][0:2, :], AF.Sqrt,
                                       scale=1.0 / HD, bias=eps_sb[0:2, 1:2])
            b.bank_free[2] = [(b.sSE, b.sSE.tick(ins))]
            nc.vector.wait_ge(b.sSE.h, b.sSE.n)
            with nc.allow_low_precision(reason="bf16 broadcast operand (sanctioned bf16 matmul inputs)"):
                ins = nc.vector.reciprocal(rrowb[:], rrow[:, par, :])
            rr_done = b.sVE.tick(ins)
            b.pe_wait_bank(3)
            nc.tensor.wait_ge(b.sVE.h, rr_done)
            ins = nc.tensor.matmul(b.PB[3][:], expander[:], rrowb[:], start=True, stop=True)
            bc_done = b.sPE.tick(ins)
            nc.vector.wait_ge(b.sPE.h, bc_done)
            ins = nc.vector.tensor_mul(dst[:, oc, ts:ts + TC], qpre[:, par, :], b.PB[3][:])
            b.evac_marks_bank(ins, 3, b.sVE)

        for oc in range(NFC):
            for tc in range(NTC):
                it = oc * NTC + tc
                par = it % 2
                ts = tc * TC
                bk = it % 2
                b.pe_wait_bank(bk)
                for kc in range(NFC):
                    ins = nc.tensor.matmul(
                        b.PB[bk][:], wslot_ap[:, kc, oc * 128:(oc + 1) * 128],
                        xn2[:, kc, ts:ts + TC], start=(kc == 0), stop=(kc == 7),
                    )
                mm_done = b.sPE.tick(ins)
                nc.scalar.wait_ge(b.sPE.h, mm_done)
                ins = nc.scalar.activation(sq2[:, par, :], b.PB[bk][:], AF.Square, bias=bcol[:, oc:oc + 1])
                sq2_done = b.sSE.tick(ins)
                nc.vector.wait_ge(b.sSE.h, sq2_done)
                if spill_wait is not None:
                    nc.vector.wait_ge(b.qS.h, b.qS.n)
                    spill_wait = None
                ins = nc.vector.tensor_scalar_add(qpre[:, par, :], b.PB[bk][:], bcol[:, oc:oc + 1])
                qpre_done = b.sVE.tick(ins)
                b.bank_free[bk] = [(b.sVE, qpre_done)]
                if pend is not None:
                    epilogue(pend)
                pend = (oc, tc, bk, par, sq2_done)
        epilogue(pend)

    nc.tensor.wait_ge(b.qW[0].h, b.qW[0].n)
    nc.tensor.wait_ge(b.qS.h, b.qS.n)
    nc.tensor.wait_ge(b.qC.h, b.qC.n)
    proj_rms(Qh, wsl_q, bqq_sb, eqn_sb, spill)
    q_use = b.sPE.n
    if until == "Aq":
        b.barrier()
        return dict(xn2=xn2, x1_d=x1_d, s_d=s_d)
    for tc in range(NTC):
        ts = tc * TC
        b.pe_wait_bank(4)
        for kc in range(NFC):
            ins = nc.tensor.matmul(
                b.PB[4][0:16, :], wsl_q[:, kc, 1024:1040], xn2[:, kc, ts:ts + TC],
                start=(kc == 0), stop=(kc == 7),
            )
        gmm = b.sPE.tick(ins)
        nc.scalar.wait_ge(b.sPE.h, gmm)
        ins = nc.scalar.activation(G_sb[:, ts:ts + TC], b.PB[4][0:16, :], AF.Sigmoid,
                                   bias=bqg_sb[:])
        b.bank_free[4] = [(b.sSE, b.sSE.tick(ins))]
    g_use = b.sPE.n
    ins = nc.sync.dma_start(out=g_d, in_=G_sb[:])
    ins._wait_ge(b.sSE.h, b.sSE.n)
    b.qS.tick(ins, 16)
    if until == "Ag":
        nc.sync.wait_ge(b.qS.h, b.qS.n)
        b.barrier()
        return dict(xn2=xn2, x1_d=x1_d, s_d=s_d)
    nc.tensor.wait_ge(b.qW[1].h, b.qW[1].n)
    proj_rms(Kh, wsl_1, bkk_sb, ekn_sb, None)
    k_use = b.sPE.n
    if until == "Ak":
        b.barrier()
        return dict(xn2=xn2, x1_d=x1_d, s_d=s_d)
    b.qW[0].tick(nc.gpsimd.dma_start(
        out=wsl_v[:], in_=wv.rearrange("(k p) o -> p k o", p=128))._wait_ge(b.sPE.h, max(q_use, g_use)), 16)
    b.qW[1].tick(nc.gpsimd.dma_start(
        out=wsl_1[:], in_=wo.rearrange("(k p) o -> p k o", p=128))._wait_ge(b.sPE.h, k_use), 16)
    nc.tensor.wait_ge(b.qW[0].h, b.qW[0].n)
    vst_free = [0, 0]
    for tt in range(16):
        buf = tt % 2
        for half in range(2):
            bk = 5 + half
            b.pe_wait_bank(bk)
            for kc in range(NFC):
                ins = nc.tensor.matmul(
                    b.PB[bk][:], xn2[:, kc, tt * 128:(tt + 1) * 128],
                    wsl_v[:, kc, half * 512:(half + 1) * 512],
                    start=(kc == 0), stop=(kc == 7),
                )
            b.sPE.tick(ins)
        vmm = b.sPE.n
        if vst_free[buf]:
            nc.vector.wait_ge(b.qV[buf].h, b.qV[buf].n)
        for half in range(2):
            ins = nc.vector.scalar_tensor_tensor(
                vstage[:, buf, half * 8:(half + 1) * 8, 0:64],
                b.PB[5 + half].rearrange("p (a c) -> p a c", c=64),
                1.0,
                bvp_sb[:, half * 8:(half + 1) * 8, :],
                ALU.mult, ALU.add,
            )
            if half == 0:
                ins._wait_ge(b.sPE.h, vmm)
            b.evac_marks_bank(ins, 5 + half, b.sVE)
        ins = nc.vector.tensor_copy(
            vstage[:, buf, :, 64:65], ones16b_sb.rearrange("p (a o) -> p a o", o=1))
        b.sVE.tick(ins)
        vwr = b.sVE.n
        ins = nc.sync.dma_start(out=v_d[tt], in_=vstage[:, buf])
        ins._wait_ge(b.sVE.h, vwr)
        vst_free[buf] = b.qV[buf].tick(ins, 16)
    nc.sync.wait_ge(b.qV[0].h, b.qV[0].n)
    nc.sync.wait_ge(b.qV[1].h, b.qV[1].n)
    nc.sync.wait_ge(b.qS.h, b.qS.n)
    b.barrier()
    dbg("Qh", Qh[:])
    dbg("Kh", Kh[:])
    if until == "A":
        return dict(xn2=xn2, x1_d=x1_d, s_d=s_d)

    for h in range(NH):
        rb = (h % 2) * 64
        hc = h // 2
        for qc in range(16):
            bs = 4 * (qc % 2)
            for j in range(4):
                b.pe_wait_bank(bs + j)
                ins = nc.tensor.matmul(
                    b.PB[bs + j][:],
                    Qh[rb:rb + 64, hc, qc * 128:(qc + 1) * 128],
                    Kh[rb:rb + 64, hc, j * 512:(j + 1) * 512],
                    start=True, stop=True,
                )
            smm = b.sPE.tick(ins)
            nc.vector.wait_ge(b.sPE.h, smm)
            for j in range(4):
                ins = nc.vector.reduce_max(m4[:, j:j + 1], b.PB[bs + j][:], axis=AX)
                b.evac_marks_bank(ins, bs + j, b.sVE)
            nc.vector.wait_ge(b.sVE.h, b.sVE.n)
            ins = nc.vector.reduce_max(m4[:, 4:5], m4[:, 0:4], axis=AX)
            nc.vector.wait_ge(b.sVE.h, b.sVE.tick(ins))
            ins = nc.vector.tensor_scalar_mul(mxneg[:, h * 16 + qc:h * 16 + qc + 1], m4[:, 4:5], -1.0)
            b.sVE.tick(ins)
    b.barrier()
    if until == "B1":
        return dict(xn2=xn2, x1_d=x1_d, s_d=s_d)
    for blk in range(2):
        ins = nc.tensor.transpose(b.PB[blk][:, 0:128], mxneg[:, blk * 128:(blk + 1) * 128], idn_sb[:])
        t_done = b.sPE.tick(ins)
        ins = nc.vector.tensor_copy(mxT[:, blk, :], b.PB[blk][:, 0:128])
        ins._wait_ge(b.sPE.h, t_done)
        b.evac_marks_bank(ins, blk, b.sVE)
    ins = nc.sync.dma_start(out=mx_d.rearrange("(a p) q -> p a q", p=128), in_=mxT[:])
    ins._wait_ge(b.sVE.h, b.sVE.n)
    b.qS.tick(ins, 16)
    nc.sync.wait_ge(b.qS.h, b.qS.n)
    b.barrier()
    if until == "B2":
        return dict(xn2=xn2, x1_d=x1_d, s_d=s_d)

    at_store = [0]
    ins = nc.sync.dma_start(out=Kx[64:65, :], in_=ones2048b)
    b.qS.tick(ins, 16)
    for h in range(NH):
        rb = (h % 2) * 64
        hcc = h // 2
        b.qS.tick(nc.sync.dma_start(out=Qx[0:64, :], in_=Qh[rb:rb + 64, hcc, :]), 16)
        b.qS.tick(nc.sync.dma_start(out=Kx[0:64, :], in_=Kh[rb:rb + 64, hcc, :]), 16)
        b.qG.tick(nc.gpsimd.dma_start(
            out=Qx[64:65, :], in_=mx_d[h * 16:(h + 1) * 16, :].rearrange("a q -> (a q)")[None, :]), 16)
        b.qS.tick(nc.sync.dma_start(out=g64[64:65, :], in_=g_d[h:h + 1, :]), 16)
        b.qS.tick(nc.sync.dma_start(
            out=vslab[:, 0], in_=v_d[:, :, h, :].rearrange("t p c -> p t c")), 16)
        nc.tensor.wait_ge(b.qG.h, b.qG.n)
        nc.tensor.wait_ge(b.qS.h, b.qS.n)
        nc.vector.wait_ge(b.qS.h, b.qS.n)
        for qgp in range(2):
            exp_tick = {}
            for kc in range(17):
                if kc < 16:
                    par = kc % 2
                    for u in range(2):
                        qg = 2 * qgp + u
                        b.pe_wait_bank(2 * par + u)
                        ins = nc.tensor.matmul(
                            b.PB[2 * par + u][:],
                            Kx[:, kc * 128:(kc + 1) * 128],
                            Qx[:, qg * 512:(qg + 1) * 512],
                            start=True, stop=True,
                        )
                    smm = b.sPE.tick(ins)
                    nc.scalar.wait_ge(b.sPE.h, smm)
                    for u in range(2):
                        ins = nc.scalar.activation(
                            PT[:, par * 1024 + u * 512:par * 1024 + (u + 1) * 512],
                            b.PB[2 * par + u][:], AF.Exp, scale=0.125,
                        )
                        b.evac_marks_bank(ins, 2 * par + u, b.sSE)
                    exp_tick[kc] = b.sSE.n
                if kc >= 1:
                    ko = kc - 1
                    po = ko % 2
                    nc.tensor.wait_ge(b.sSE.h, exp_tick[ko])
                    if ko == 0:
                        b.pe_wait_bank(4)
                        b.pe_wait_bank(5)
                    for u in range(2):
                        ins = nc.tensor.matmul(
                            b.PB[4 + u][0:65, :], vslab[:, 0, ko, :],
                            PT[:, po * 1024 + u * 512:po * 1024 + (u + 1) * 512],
                            start=(ko == 0), stop=(ko == 15),
                        )
                    b.sPE.tick(ins)
            acc_done = b.sPE.n
            for u in range(2):
                qg = 2 * qgp + u
                nc.vector.wait_ge(b.sPE.h, acc_done)
                ins = nc.vector.reciprocal(rs64[64:65, :], b.PB[4 + u][64:65, :])
                nc.vector.wait_ge(b.sVE.h, b.sVE.tick(ins))
                with nc.allow_low_precision(reason="bf16 broadcast operand (sanctioned bf16 matmul inputs)"):
                    ins = nc.vector.tensor_mul(rs64b[64:65, :], rs64[64:65, :],
                                               g64[64:65, qg * 512:(qg + 1) * 512])
                b.sVE.tick(ins)
                b.pe_wait_bank(6)
                nc.tensor.wait_ge(b.sVE.h, b.sVE.n)
                ins = nc.tensor.matmul(b.PB[6][0:64, :], ones65b_sb[64:65, 0:64],
                                       rs64b[64:65, :], start=True, stop=True)
                bc_done = b.sPE.tick(ins)
                nc.vector.wait_ge(b.sPE.h, bc_done)
                ins = nc.vector.tensor_copy(bc2m[0:64, :], b.PB[6][0:64, :])
                b.evac_marks_bank(ins, 6, b.sVE)
                if at_store[0]:
                    nc.vector.wait_ge(b.qA.h, at_store[0])
                ins = nc.vector.tensor_mul(atst[:], b.PB[4 + u][0:64, :], bc2m[0:64, :])
                b.evac_marks_bank(ins, 4 + u, b.sVE)
                at_wr = b.sVE.n
                ins = nc.sync.dma_start(
                    out=at_d[rb:rb + 64, hcc, qg * 512:(qg + 1) * 512], in_=atst[:]
                )
                ins._wait_ge(b.sVE.h, at_wr)
                at_store[0] = b.qA.tick(ins, 16)
        b.barrier()

    if until == "B3":
        return dict(xn2=xn2, x1_d=x1_d, s_d=s_d)
    nc.sync.wait_ge(b.qA.h, b.qA.n)
    nc.tensor.wait_ge(b.qW[1].h, b.qW[1].n)
    st_free = [0, 0]
    for tq in range(NTC):
        ts = tq * TC
        asl = b.qS.tick(nc.sync.dma_start(out=at_slab[:], in_=at_d[:, :, ts:ts + TC]), 16)
        nc.tensor.wait_ge(b.qS.h, b.qS.n)
        for oc in range(NFC):
            bk = oc % 2
            sbuf = oc % 2
            b.pe_wait_bank(bk)
            for kc in range(NFC):
                ins = nc.tensor.matmul(
                    b.PB[bk][:], wsl_1[:, kc, oc * 128:(oc + 1) * 128],
                    at_slab[:, kc, :], start=(kc == 0), stop=(kc == 7),
                )
            mm_done = b.sPE.tick(ins)
            if st_free[sbuf]:
                nc.vector.wait_ge(b.qT[sbuf].h, b.qT[sbuf].n)
            nc.vector.wait_ge(b.sPE.h, mm_done)
            ins = nc.vector.tensor_scalar_add(sstage[:, sbuf, :], b.PB[bk][:], boc_sb[:, oc:oc + 1])
            b.evac_marks_bank(ins, bk, b.sVE)
            swr = b.sVE.n
            ins = nc.sync.dma_start(out=s_d[:, oc, ts:ts + TC], in_=sstage[:, sbuf, :])
            ins._wait_ge(b.sVE.h, swr)
            st_free[sbuf] = b.qT[sbuf].tick(ins, 16)
        b.barrier()

    return dict(xn2=xn2, x1_d=x1_d, s_d=s_d)


def l1_host_inputs(ins):
    c = np.ascontiguousarray
    qn = np.asarray(ins["L1_attn_qn_w"], np.float32)
    kn = np.asarray(ins["L1_attn_kn_w"], np.float32)
    eqn = np.zeros((2, 128), np.float32)
    ekn = np.zeros((2, 128), np.float32)
    for j in range(2):
        eqn[j, j * 64:(j + 1) * 64] = qn
        ekn[j, j * 64:(j + 1) * 64] = kn
    mask2 = np.zeros((128, 2), np.float32)
    mask2[0:64, 0] = 1.0
    mask2[64:128, 1] = 1.0
    bv = np.asarray(ins["L1_attn_bv"], np.float32)
    bvpad = np.broadcast_to(bv.reshape(16, 64), (128, 16, 64))
    return {
        "L1_wq": c(ins["L1_attn_Wq"]),
        "L1_wk": c(ins["L1_attn_Wk"]),
        "L1_wv": c(ins["L1_attn_Wv"]),
        "L1_wo": c(ins["L1_attn_Wo"]),
        "L1_bqq": c(np.asarray(ins["L1_attn_bq"], np.float32)[:1024].reshape(8, 128).T),
        "L1_bqg": c(np.asarray(ins["L1_attn_bq"], np.float32)[1024:].reshape(16, 1)),
        "L1_bkk": c(np.asarray(ins["L1_attn_bk"], np.float32).reshape(8, 128).T),
        "L1_bvpad": c(bvpad.astype(BF)),
        "L1_boc": c(np.asarray(ins["L1_attn_bo"], np.float32).reshape(8, 128).T),
        "L1_eqn": eqn.astype(BF),
        "L1_ekn": ekn.astype(BF),
        "L1_mask2": mask2.astype(BF),
        "C_one1b": np.ones((1, 128), BF),
        "C_ones65f": np.concatenate([np.zeros((64, 64), np.float32), np.ones((1, 64), np.float32)], 0),
        "C_ones2048b": np.ones((1, 2048), BF),
        "C_ones16b": np.ones((128, 16), BF),
        "C_ones65b": np.concatenate([np.zeros((64, 64), np.float32), np.ones((1, 64), np.float32)], 0).astype(BF),
    }


MOE_TP = 4
MOE_TPW = 512

def emit_moe(b: Build, c, xn2, debug=None, moe_d=None):
    nc = b.nc
    dbg = c["dbg"]
    one1_sb = c["one1_sb"]
    idn_sb = c["idn_sb"]

    wg = b.din("L1_wg", (D, 12))
    wup = b.din("L1_wup", (12, 32, D, 128))
    wdn = b.din("L1_wdn", (12, 4096, D))
    bup = b.din("L1_bup", (128, 12, 32))
    bdn = b.din("L1_bdn", (12, D))

    if moe_d is None:
        moe_d = b.dscratch("moe_d", (128, NFC, T))
    cw_d = b.dscratch("cw_d", (12, T))

    TPW = 1024
    wdn_bf = b.sba("wdn_bf", [128, 32, KB], BF16, OFF_BIG)
    hA = b.sba("hA", [128, 16, TPW], BF16, OFF_SLOT0)
    hB = b.sba("hB", [128, 8, TPW], BF16, 24576)
    hC = b.sba("hC", [128, 3, TPW], BF16, 40960)
    hD = b.sba("hD", [128, 4, TPW], BF16, 84992)
    hE = b.sba("hE", [128, 1, TPW], BF16, 144384)

    def h_sl(hc, th=None):
        for t_, base, n_ in ((hA, 0, 16), (hB, 16, 8), (hC, 24, 3), (hD, 27, 4), (hE, 31, 1)):
            if hc < base + n_:
                if th is None:
                    return t_[:, hc - base, :]
                return t_[:, hc - base, th * 512:(th + 1) * 512]
        raise AssertionError

    wup_pan = b.sba("wup_pan", [128, 2, 2, NFC, 128], BF16, 125952)
    cwb = b.sba("cwb", [128, 512], F32, 134144)
    mtmp = b.sba("mtmp", [128, 2, 512], F32, 136192)
    cwrow = b.sba("cwrow", [1, T], BF16, 140288)
    bup_sb = b.sba("bup_sb", [128, 12, 32], F32, 47104)
    mtop = b.sba("mtop", [128, 6, 12], F32, 48640)
    mcat8 = b.sba("mcat8", [128, 8], F32, 48928)
    cw_tok = b.sba("cw_tok", [128, 12], F32, 48960)
    wg_sb = b.sba("wg_sb", [128, NFC, 12], BF16, OFF_CONST + 1568)
    cwT_sb = b.sba("cwT_sb", [12, T], F32, OFF_BIG)
    bdn_sb = b.sba("bdn_sb", [12, KB], F32, OFF_BIG + 8192)

    qWD = S(nc, "qWD")
    qU = [S(nc, "qU0"), S(nc, "qU1")]
    qM = [S(nc, "qM0"), S(nc, "qM1")]
    qF = S(nc, "qF")

    for src, dst in ((bup, bup_sb), (bdn, bdn_sb)):
        qF.tick(nc.sync.dma_start(out=dst[:], in_=src), 16)
    b.qG.tick(nc.gpsimd.dma_start(out=wg_sb[:], in_=wg.rearrange("(k p) o -> p k o", p=128)), 16)
    nc.tensor.wait_ge(qF.h, qF.n)
    nc.tensor.wait_ge(b.qG.h, b.qG.n)

    for tg in range(16):
        b.pe_wait_bank(0)
        for kc in range(NFC):
            ins = nc.tensor.matmul(
                b.PB[0][:, 0:12], xn2[:, kc, tg * 128:(tg + 1) * 128], wg_sb[:, kc, :],
                start=(kc == 0), stop=(kc == 7),
            )
        gmm = b.sPE.tick(ins)

        def vs(ins):
            v = b.sVE.tick(ins)
            nc.vector.wait_ge(b.sVE.h, v)
            return v
        nc.vector.wait_ge(b.sPE.h, gmm)
        ins = nc.vector.tensor_copy(mtop[:, 0, :], b.PB[0][:, 0:12])
        b.evac_marks_bank(ins, 0, b.sVE)
        nc.vector.wait_ge(b.sVE.h, b.sVE.n)
        L = mtop[:, 0, :]
        vs(nc.vector.reduce_max(mcat8[:, 0:1], L, axis=AX))
        vs(nc.vector.tensor_single_scalar(mtop[:, 1, :], L, mcat8[:, 0:1], ALU.is_equal))
        vs(nc.vector.scalar_tensor_tensor(mtop[:, 4, :], mtop[:, 1, :], -1e30, L, ALU.mult, ALU.add))
        vs(nc.vector.reduce_max(mcat8[:, 1:2], mtop[:, 4, :], axis=AX))
        vs(nc.vector.tensor_single_scalar(mtop[:, 2, :], mtop[:, 4, :], mcat8[:, 1:2], ALU.is_equal))
        vs(nc.vector.scalar_tensor_tensor(mtop[:, 5, :], mtop[:, 2, :], -1e30, mtop[:, 4, :], ALU.mult, ALU.add))
        vs(nc.vector.reduce_max(mcat8[:, 2:3], mtop[:, 5, :], axis=AX))
        vs(nc.vector.tensor_single_scalar(mtop[:, 3, :], mtop[:, 5, :], mcat8[:, 2:3], ALU.is_equal))
        ve1 = vs(nc.vector.tensor_scalar_mul(mcat8[:, 3:4], mcat8[:, 0:1], -1.0))
        nc.scalar.wait_ge(b.sVE.h, ve1)
        ins = nc.scalar.activation(mcat8[:, 4:7], mcat8[:, 0:3], AF.Exp, bias=mcat8[:, 3:4])
        se1 = b.sSE.tick(ins)
        nc.vector.wait_ge(b.sSE.h, se1)
        vs(nc.vector.reduce_sum(mcat8[:, 7:8], mcat8[:, 4:7], axis=AX))
        vs(nc.vector.reciprocal(mcat8[:, 7:8], mcat8[:, 7:8]))
        vs(nc.vector.tensor_scalar(cw_tok[:], mtop[:, 1, :], mcat8[:, 4:5], mcat8[:, 7:8], ALU.mult, ALU.mult))
        vs(nc.vector.tensor_scalar(mtop[:, 4, :], mtop[:, 2, :], mcat8[:, 5:6], mcat8[:, 7:8], ALU.mult, ALU.mult))
        vs(nc.vector.tensor_add(cw_tok[:], cw_tok[:], mtop[:, 4, :]))
        vs(nc.vector.tensor_scalar(mtop[:, 5, :], mtop[:, 3, :], mcat8[:, 6:7], mcat8[:, 7:8], ALU.mult, ALU.mult))
        ins = nc.vector.tensor_add(cw_tok[:], cw_tok[:], mtop[:, 5, :])
        cw_done = b.sVE.tick(ins)
        nc.vector.wait_ge(b.sVE.h, cw_done)
        b.pe_wait_bank(1)
        nc.tensor.wait_ge(b.sVE.h, cw_done)
        ins = nc.tensor.transpose(b.PB[1][0:12, 0:128], cw_tok[:], idn_sb[:])
        t_done = b.sPE.tick(ins)
        nc.vector.wait_ge(b.sPE.h, t_done)
        ins = nc.vector.tensor_copy(cwT_sb[:, tg * 128:(tg + 1) * 128], b.PB[1][0:12, 0:128])
        b.evac_marks_bank(ins, 1, b.sVE)
    ins = nc.sync.dma_start(out=cw_d, in_=cwT_sb[:])
    ins._wait_ge(b.sVE.h, b.sVE.n)
    qF.tick(ins, 16)
    nc.sync.wait_ge(qF.h, qF.n)
    b.barrier()
    dbg("cw", cwT_sb[:])

    for oc in range(NFC):
        for tq in range(NTC):
            bk = tq % 2
            buf = tq % 2
            b.pe_wait_bank(bk)
            ins = nc.tensor.matmul(
                b.PB[bk][:], bdn_sb[:, oc * 128:(oc + 1) * 128],
                cwT_sb[:, tq * TC:(tq + 1) * TC], start=True, stop=True,
            )
            mm_done = b.sPE.tick(ins)
            nc.vector.wait_ge(qM[buf].h, qM[buf].n)
            nc.vector.wait_ge(b.sPE.h, mm_done)
            ins = nc.vector.tensor_copy(mtmp[:, buf, :], b.PB[bk][:])
            b.evac_marks_bank(ins, bk, b.sVE)
            ins = nc.gpsimd.dma_start(out=moe_d[:, oc, tq * TC:(tq + 1) * TC], in_=mtmp[:, buf, :])
            ins._wait_ge(b.sVE.h, b.sVE.n)
            qM[buf].tick(ins, 16)
    bdsum_pe = b.sPE.n

    phaseB_done = bdsum_pe
    cwb_done = 0
    pan_use = [0, 0]
    for e in range(12):
        ins = nc.gpsimd.dma_start(out=cwrow[:], in_=cw_d[e:e + 1, :])
        if cwb_done:
            ins._wait_ge(b.sPE.h, cwb_done)
        b.qG.tick(ins, 16)
        wdn_waited = False
        hoist_skip = 0
        for tp in range(2):
            t0 = tp * TPW
            gelu_guard = phaseB_done
            for hc2 in range(16):
                if tp == 0 and hc2 % 2 == 1:
                    j = hc2 // 2
                    ins = nc.gpsimd.dma_start(
                        out=wdn_bf[:, 4 * j:4 * j + 4, :],
                        in_=wdn[e, 512 * j:512 * (j + 1)].rearrange("(k p) o -> p k o", p=128))
                    if j == 0:
                        ins._wait_ge(b.sPE.h, phaseB_done)
                    qWD.tick(ins, 16)
                pbuf = hc2 % 2
                if hc2 >= hoist_skip:
                    ins = nc.gpsimd.dma_start(
                        out=wup_pan[:, pbuf],
                        in_=wup[e, 2 * hc2:2 * hc2 + 2].rearrange("a (k p) h -> p a k h", p=128))
                    if pan_use[pbuf]:
                        ins._wait_ge(b.sPE.h, pan_use[pbuf])
                    qU[pbuf].tick(ins, 16)
                nc.tensor.wait_ge(qU[pbuf].h, qU[pbuf].n)
                for sub in range(2):
                    hc = 2 * hc2 + sub
                    for th in range(2):
                        bk = th
                        b.pe_wait_bank(bk)
                        for kc in range(NFC):
                            ins = nc.tensor.matmul(
                                b.PB[bk][:], wup_pan[:, pbuf, sub, kc, :],
                                xn2[:, kc, t0 + th * 512:t0 + (th + 1) * 512],
                                start=(kc == 0), stop=(kc == 7),
                            )
                        amm = b.sPE.tick(ins)
                        nc.scalar.wait_ge(b.sPE.h, max(amm, gelu_guard))
                        gelu_guard = 0
                        ins = nc.scalar.activation(
                            h_sl(hc, th), b.PB[bk][:], AF.Gelu, bias=bup_sb[:, e, hc:hc + 1])
                        b.evac_marks_bank(ins, bk, b.sSE)
                pan_use[pbuf] = b.sPE.n
            a_done_se = b.sSE.n
            ne = e if tp == 0 else e + 1
            if ne < 12:
                for hc2p in range(2):
                    pbuf = hc2p % 2
                    ins = nc.gpsimd.dma_start(
                        out=wup_pan[:, pbuf],
                        in_=wup[ne, 2 * hc2p:2 * hc2p + 2].rearrange("a (k p) h -> p a k h", p=128))
                    if pan_use[pbuf]:
                        ins._wait_ge(b.sPE.h, pan_use[pbuf])
                    qU[pbuf].tick(ins, 16)
                hoist_skip = 2
            else:
                hoist_skip = 0
            if not wdn_waited:
                nc.tensor.wait_ge(qWD.h, qWD.n)
                nc.tensor.wait_ge(qF.h, qF.n)
                wdn_waited = True
            nc.tensor.wait_ge(b.sSE.h, a_done_se)
            for th in range(2):
                b.pe_wait_bank(4)
                ins = nc.tensor.matmul(b.PB[4][:], c["one1b_sb"],
                                       cwrow[0:1, t0 + th * 512:t0 + (th + 1) * 512],
                                       start=True, stop=True)
                cwb_done = b.sPE.tick(ins)
                nc.vector.wait_ge(b.sPE.h, cwb_done)
                ins = nc.vector.tensor_copy(cwb[:], b.PB[4][:])
                b.evac_marks_bank(ins, 4, b.sVE)
                cwb_rdy = b.sVE.n
                nc.tensor.wait_ge(b.sVE.h, cwb_rdy)
                for oc in range(NFC):
                    bk = 2 + oc % 2
                    buf = oc % 2
                    b.pe_wait_bank(bk)
                    for hc in range(32):
                        ins = nc.tensor.matmul(
                            b.PB[bk][:], wdn_bf[:, hc, oc * 128:(oc + 1) * 128], h_sl(hc, th),
                            start=(hc == 0), stop=(hc == 31),
                        )
                    bmm = b.sPE.tick(ins)
                    if oc % 2 == 0:
                        nc.vector.wait_ge(qM[0].h, qM[0].n)
                    nc.vector.wait_ge(b.sPE.h, bmm)
                    ins = nc.vector.tensor_mul(mtmp[:, buf, :], b.PB[bk][:], cwb[:])
                    b.evac_marks_bank(ins, bk, b.sVE)
                    if oc % 2 == 1:
                        ins = nc.gpsimd.dma_start(
                            out=moe_d[:, oc - 1:oc + 1, t0 + th * 512:t0 + (th + 1) * 512],
                            in_=mtmp.rearrange("p a c -> p a c"), accum_op=ALU.add)
                        ins._wait_ge(b.sVE.h, b.sVE.n)
                        qM[0].tick(ins, 16)
            phaseB_done = b.sPE.n
    nc.sync.wait_ge(qM[0].h, qM[0].n)
    nc.sync.wait_ge(qM[1].h, qM[1].n)
    b.barrier()
    return dict(moe_d=moe_d)




def emit_final(b: Build, c, x1_d, s_d, moe_d, y_out):
    nc = b.nc
    idn_sb = c["idn_sb"]
    x1s = b.sba("x1s", [128, NFC, TC], F32, 51200)
    ss = b.sba("ss", [128, NFC, TC], F32, 67584)
    ms = b.sba("ms", [128, NFC, TC], F32, 84992)
    ystage = b.sba("ystage", [128, 2, KB], F32, 101376)
    qY = S(nc, "qY")
    qL = S(nc, "qL")
    nc.sync.wait_ge(b.qT[0].h, b.qT[0].n)
    nc.sync.wait_ge(b.qT[1].h, b.qT[1].n)
    nc.sync.wait_ge(b.qS.h, b.qS.n)
    b.barrier()
    for tc in range(NTC):
        ts = tc * TC
        ins = nc.sync.dma_start(out=x1s[:], in_=x1_d[:, :, ts:ts + TC])
        if tc:
            ins._wait_ge(b.sVE.h, b.sVE.n)
        qL.tick(ins, 16)
        ins = nc.sync.dma_start(out=ss[:], in_=s_d[:, :, ts:ts + TC])
        if tc:
            ins._wait_ge(b.sVE.h, b.sVE.n)
        qL.tick(ins, 16)
        ins = nc.sync.dma_start(out=ms[:], in_=moe_d[:, :, ts:ts + TC])
        if tc:
            ins._wait_ge(b.sPE.h, b.sPE.n)
        ld = qL.tick(ins, 16)
        nc.scalar.wait_ge(qL.h, ld)
        half_done = []
        for hf in range(2):
            sl = slice(hf * 4, (hf + 1) * 4)
            ins = nc.scalar.activation(ms[:, sl, :], ms[:, sl, :], AF.Silu)
            se_done = b.sSE.tick(ins)
            nc.vector.wait_ge(b.sSE.h, se_done)
            nc.vector.tensor_mul(ms[:, sl, :], ms[:, sl, :], ss[:, sl, :])
            ins = nc.vector.tensor_add(ms[:, sl, :], ms[:, sl, :], x1s[:, sl, :])
            half_done.append(b.sVE.tick(ins))
        for ttt in range(4):
            ybuf = ttt % 2
            bks = (0, 1) if ttt % 2 == 0 else (2, 3)
            b.pe_wait_bank(bks[0])
            b.pe_wait_bank(bks[1])
            for fc in range(NFC):
                if fc == 0:
                    nc.tensor.wait_ge(b.sVE.h, half_done[0])
                if fc == 4:
                    nc.tensor.wait_ge(b.sVE.h, half_done[1])
                ins = nc.tensor.transpose(
                    b.PB[bks[fc // 4]][:, (fc % 4) * 128:(fc % 4 + 1) * 128],
                    ms[:, fc, ttt * 128:(ttt + 1) * 128],
                    idn_sb[:],
                )
            t_done = b.sPE.tick(ins)
            nc.vector.wait_ge(b.sPE.h, t_done)
            nc.vector.wait_ge(qY.h, qY.n)
            for j, bi in enumerate(bks):
                ins = nc.vector.tensor_copy(
                    ystage[:, ybuf, j * 512:(j + 1) * 512], b.PB[bi][:])
                b.evac_marks_bank(ins, bi, b.sVE)
            ywr = b.sVE.n
            row0 = ts + ttt * 128
            ins = nc.sync.dma_start(out=y_out[row0:row0 + 128, :], in_=ystage[:, ybuf, :])
            ins._wait_ge(b.sVE.h, ywr)
            qY.tick(ins, 16)
    nc.sync.wait_ge(qY.h, qY.n)
    return qY


def moe_host_inputs(ins):
    c = np.ascontiguousarray
    wup = np.asarray(ins["L1_ffn_Wup"], np.float32)
    bup = np.asarray(ins["L1_ffn_bup"], np.float32)
    return {
        "L1_wg": c(np.asarray(ins["L1_ffn_Wg"], np.float32)),
        "L1_wup": c(wup.reshape(12, 1024, 32, 128).transpose(0, 2, 1, 3)),
        "L1_wdn": c(np.asarray(ins["L1_ffn_Wdown"], np.float32)),
        "L1_bup": c(bup.reshape(12, 32, 128).transpose(2, 0, 1)),
        "L1_bdn": c(np.asarray(ins["L1_ffn_bdown"], np.float32)),
    }



def _host_inputs(ins):
    h = l0_host_inputs(ins)
    h.update(l1_host_inputs(ins))
    h.update(moe_host_inputs(ins))
    return h


def _build_full():
    b = Build()
    x_in = b.din("x_in", (2048, 1024))
    c = emit_l0(b, x_in)
    r1 = emit_l1_attn(b, c)
    r2 = emit_moe(b, c, r1["xn2"])
    y = b.dout("y", (2048, 1024))
    emit_final(b, c, r1["x1_d"], r1["s_d"], r2["moe_d"], y)
    return b


def kernel(**inputs):
    ins = {k: np.asarray(v) for k, v in inputs.items()}
    host = _host_inputs(ins)
    b = _build_full()
    x = np.asarray(ins["L0_attn_x"], np.float32)
    in_maps = []
    for core in range(8):
        m = dict(host)
        m["x_in"] = np.ascontiguousarray(x[core])
        in_maps.append(m)
    res = run_bass_kernel_spmd(b.nc, in_maps, core_ids=list(range(8)))
    out = np.stack([r["y"] for r in res.results], axis=0).astype(np.float32)
    return out


# version 09
# speedup vs baseline: 1.0503x; 1.0503x over previous
import numpy as np
import ml_dtypes
import concourse.bass as bass
import concourse.mybir as mybir
from concourse.bass_utils import run_bass_kernel_spmd

F32 = mybir.dt.float32
BF16 = mybir.dt.bfloat16
AX = mybir.AxisListType.X
AF = mybir.ActivationFunctionType
ALU = mybir.AluOpType
BF = ml_dtypes.bfloat16

D = 1024
T = 2048
NFC = 8
NH = 16
HD = 64
TC = 512
NTC = T // TC
KB = 1024
AQ_ITERS = None
AQ_PLAIN = False
AQ_EPI = 99

SB_CEIL = 212864
OFF_AROW = 24576
OFF_BROW = 32768
OFF_RTMP = 40960
OFF_AB = 45056
OFF_CONST = 49152
OFF_SLOT0 = 51200
OFF_SLOT1 = 68096
OFF_T = 84992
OFF_BIG = 147328

T_SQ = OFF_T + 0
T_XN = OFF_T + 8192
T_H1 = OFF_T + 16384
T_H2 = OFF_T + 24576
T_Q = OFF_T + 32768
T_AT = OFF_T + 40960
T_KV = OFF_T + 49152
T_KVB = OFF_T + 51200
T_G = OFF_T + 52224
T_VT = OFF_T + 54272


class S:
    def __init__(self, nc, name):
        self.h = nc.alloc_semaphore(name)
        self.n = 0

    def tick(self, ins, k=1):
        ins.then_inc(self.h, k)
        self.n += k
        return self.n


class Build:
    def __init__(self):
        nc = bass.Bass("TRN2", target_bir_lowering=False)
        self.nc = nc
        self.input_names = []
        self.qS = S(nc, "qS")
        self.qG = S(nc, "qG")
        self.qC = S(nc, "qC")
        self.qX = [S(nc, "qX0"), S(nc, "qX1")]
        self.qW = [S(nc, "qW0"), S(nc, "qW1")]
        self.qV = [S(nc, "qV0"), S(nc, "qV1")]
        self.qA = S(nc, "qA")
        self.qT = [S(nc, "qT0"), S(nc, "qT1")]
        self.sPE = S(nc, "sPE")
        self.sVE = S(nc, "sVE")
        self.sSE = S(nc, "sSE")
        self.PB = [nc.alloc_psum_tensor(f"pb{i}", [128, 512], F32).ap() for i in range(8)]
        self.bank_free = [None] * 8
        self.ranges = []

    def din(self, name, shape, dtype=F32):
        ap = self.nc.dram_tensor(name, list(shape), dtype, kind="ExternalInput").ap()
        self.input_names.append(name)
        return ap

    def dout(self, name, shape, dtype=F32):
        return self.nc.dram_tensor(name, list(shape), dtype, kind="ExternalOutput").ap()

    def dscratch(self, name, shape, dtype=F32):
        return self.nc.dram_tensor(name, list(shape), dtype, kind="Internal").ap()

    def sba(self, name, shape, dtype, offset):
        esz = 2 if dtype == BF16 else 4
        nbytes = int(np.prod(shape[1:])) * esz
        assert offset >= 24576 and offset + nbytes <= SB_CEIL, (name, offset, nbytes)
        self.ranges.append((name, offset, offset + nbytes))
        return self.nc.alloc_sbuf_tensor_at(name, shape, dtype, offset=offset).ap()

    def pe_wait_bank(self, i):
        if self.bank_free[i]:
            for sem, val in self.bank_free[i]:
                self.nc.tensor.wait_ge(sem.h, val)
            self.bank_free[i] = None

    def evac_marks_bank(self, ins, i, sem):
        val = sem.tick(ins)
        self.bank_free[i] = [(sem, val)]
        return val

    def add_bank_reader(self, i, sem, val):
        if not self.bank_free[i]:
            self.bank_free[i] = []
        self.bank_free[i].append((sem, val))

    def barrier(self):
        self.nc.all_engine_barrier()
        self.bank_free = [None] * 8

    def audit(self):
        j = self.nc.to_json()
        mine = set()
        for nm, a0, a1 in self.ranges:
            mine.add(nm)
        bad = []
        for al in j["functions"][0]["allocations"]:
            for ml in al.get("memorylocations", []):
                if ml.get("type") != "SB":
                    continue
                a0 = ml["addr"]
                a1 = a0 + ml["dims"][1]
                nm = ml["name"]
                base = nm.split("_set")[0]
                if not any(base.startswith(r) for r in mine) and a1 > 24576:
                    bad.append((nm, a0, a1))
        assert not bad, f"auto allocs entered arena: {bad}"


def emit_l0(b: Build, x_in, debug=None):
    nc = b.nc

    def dbg(name, src_ap):
        if debug is not None and name in debug:
            v = b.qS.tick(nc.sync.dma_start(out=debug[name], in_=src_ap), 16)
            nc.sync.wait_ge(b.qS.h, v)
            b.barrier()

    wqT = b.din("L0_wqT", (D, D))
    wkT = b.din("L0_wkT", (D, D))
    wvT = b.din("L0_wvT", (D, D))
    woT = b.din("L0_woT", (D, D))
    w1 = b.din("L0_w1", (D, D))
    wa1 = b.din("L0_wa1", (D, D))
    wa2 = b.din("L0_wa2", (D, D))
    fbias = b.din("L0_fbias", (128, 8))
    ones128 = b.din("C_ones128", (128, 1))
    ones128b = b.din("C_ones128b", (128, 1), BF16)
    one1 = b.din("C_one1", (1, 128))
    idn = b.din("C_idn", (128, 128))
    eps = b.din("C_eps", (128, 2))

    x_fm = b.sba("x_fm", [128, NFC, T], F32, OFF_BIG)
    WS = [
        b.sba("wslot0", [128, NFC, KB], BF16, OFF_SLOT0),
        b.sba("wslot1", [128, NFC, KB], BF16, OFF_SLOT1),
    ]
    x_tm = b.sba("x_tm", [128, 2, KB], F32, T_H1)
    sq = b.sba("sq", [128, NFC, TC], BF16, T_SQ)
    xn = b.sba("xn", [128, NFC, TC], BF16, T_XN)
    h1_bf = b.sba("h1_bf", [128, NFC, TC], BF16, T_H1)
    h2_bf = b.sba("h2_bf", [128, NFC, TC], BF16, T_H2)
    q_bf = b.sba("q_bf", [128, NFC, TC], BF16, T_Q)
    at_bf = b.sba("at_bf", [128, NFC, TC], BF16, T_AT)
    kt_bf = b.sba("kt_bf", [128, 2, KB], BF16, T_Q)
    vt_bf = b.sba("vt_bf", [128, 2, KB], BF16, T_AT)
    kv_acc = b.sba("kv_acc", [128, 512], F32, T_KV)
    kv_sb = b.sba("kv_sb", [128, 512], BF16, T_KVB)
    g_oc = b.sba("g_oc", [128, TC], F32, T_G)
    vtmp = b.sba("vtmp", [128, TC], F32, T_VT)
    a_ln = b.sba("a_ln", [1, T], F32, OFF_AROW)
    bneg_ln = b.sba("bneg_ln", [1, T], F32, OFF_BROW)
    rowtmp = b.sba("rowtmp", [1, 2, TC], F32, OFF_RTMP)
    ab_sb = b.sba("ab_sb", [128, 2, TC], F32, OFF_AB)
    ones_sb = b.sba("ones_sb", [128, 1], F32, OFF_CONST)
    onesb_sb = b.sba("onesb_sb", [128, 1], BF16, OFF_CONST + 32)
    one1_sb = b.sba("one1_sb", [1, 128], F32, OFF_CONST + 64)
    fb_sb = b.sba("fb_sb", [128, 8], F32, OFF_CONST + 576)
    eps_sb = b.sba("eps_sb", [128, 2], F32, OFF_CONST + 608)
    idn_sb = b.sba("idn_sb", [128, 128], F32, OFF_CONST + 640)

    for src, dst in (
        (ones128, ones_sb), (ones128b, onesb_sb), (one1, one1_sb),
        (idn, idn_sb), (fbias, fb_sb), (eps, eps_sb),
    ):
        b.qC.tick(nc.sync.dma_start(out=dst[:], in_=src), 16)
    const_loaded = b.qC.n

    def cast_load(slot, src, wait_pe=0):
        ins = nc.gpsimd.dma_start(
            out=WS[slot][:], in_=src.rearrange("(k p) o -> p k o", p=128)
        )
        if wait_pe:
            ins._wait_ge(b.sPE.h, wait_pe)
        return slot, b.qW[slot].tick(ins, 16)

    cast_load(0, wkT)
    cast_load(1, wvT)

    x_tm_free = [0, 0]
    for tt in range(16):
        buf = tt % 2
        ins = nc.sync.dma_start(out=x_tm[:, buf, :], in_=x_in[tt * 128:(tt + 1) * 128, :])
        if x_tm_free[buf]:
            ins._wait_ge(b.sPE.h, x_tm_free[buf])
        b.qX[buf].tick(ins, 16)
        bk = [(0, 1), (2, 3)][buf]
        nc.tensor.wait_ge(b.qX[buf].h, b.qX[buf].n)
        if tt == 0:
            nc.tensor.wait_ge(b.qC.h, b.qC.n)
        b.pe_wait_bank(bk[0])
        b.pe_wait_bank(bk[1])
        for fc in range(NFC):
            ins = nc.tensor.transpose(
                b.PB[bk[fc // 4]][:, (fc % 4) * 128:(fc % 4 + 1) * 128],
                x_tm[:, buf, fc * 128:(fc + 1) * 128],
                idn_sb[:],
            )
        x_tm_free[buf] = b.sPE.tick(ins)
        pe_done = b.sPE.n
        for j, bi in enumerate(bk):
            ins = nc.vector.tensor_copy(
                x_fm[:, j * 4:(j + 1) * 4, tt * 128:(tt + 1) * 128],
                b.PB[bi].rearrange("p (a c) -> p a c", c=128),
            )
            if j == 0:
                ins._wait_ge(b.sPE.h, pe_done)
            b.evac_marks_bank(ins, bi, b.sVE)
    b.barrier()
    dbg("x_fm0", x_fm[:])

    def emit_ln_rows(tc):
        ts = tc * TC
        xsl = x_fm[:, :, ts:ts + TC]
        ins = nc.scalar.activation(sq[:], xsl, AF.Square)
        sq_done = b.sSE.tick(ins)
        bA, bB = 4, 5
        b.pe_wait_bank(bA)
        b.pe_wait_bank(bB)
        for fc in range(NFC):
            ins = nc.tensor.matmul(
                b.PB[bA][0:1, :], ones_sb[:], xsl[:, fc, :], start=(fc == 0), stop=(fc == 7)
            )
        sum_done = b.sPE.tick(ins)
        nc.tensor.wait_ge(b.sSE.h, sq_done)
        for fc in range(NFC):
            ins = nc.tensor.matmul(
                b.PB[bB][0:1, :], onesb_sb[:], sq[:, fc, :], start=(fc == 0), stop=(fc == 7)
            )
        sq_sum_done = b.sPE.tick(ins)
        ins = nc.scalar.mul(rowtmp[0:1, 0, :], b.PB[bA][0:1, :], 1.0 / D)
        ins._wait_ge(b.sPE.h, sum_done)
        b.bank_free[bA] = [(b.sSE, b.sSE.tick(ins))]
        ins = nc.scalar.mul(rowtmp[0:1, 1, :], b.PB[bB][0:1, :], 1.0 / D)
        ins._wait_ge(b.sPE.h, sq_sum_done)
        b.bank_free[bB] = [(b.sSE, b.sSE.tick(ins))]
        ins = nc.scalar.activation(a_ln[0:1, ts:ts + TC], rowtmp[0:1, 0, :], AF.Square)
        se_done = b.sSE.tick(ins)
        ins = nc.vector.tensor_sub(rowtmp[0:1, 1, :], rowtmp[0:1, 1, :], a_ln[0:1, ts:ts + TC])
        ins._wait_ge(b.sSE.h, se_done)
        ve_done = b.sVE.tick(ins)
        ins = nc.scalar.activation(rowtmp[0:1, 1, :], rowtmp[0:1, 1, :], AF.Sqrt, bias=eps_sb[0:1, 0:1])
        ins._wait_ge(b.sVE.h, ve_done)
        se_done = b.sSE.tick(ins)
        ins = nc.vector.reciprocal(a_ln[0:1, ts:ts + TC], rowtmp[0:1, 1, :])
        ins._wait_ge(b.sSE.h, se_done)
        b.sVE.tick(ins)
        ins = nc.vector.tensor_mul(
            bneg_ln[0:1, ts:ts + TC], rowtmp[0:1, 0, :], a_ln[0:1, ts:ts + TC]
        )
        return b.sVE.tick(ins), sq_sum_done

    def emit_xn(tc, rows_done, sq_reader_pe=0, dst_sl=None):
        if dst_sl is None:
            dst_sl = lambda fc: xn[:, fc, :]
        ts = tc * TC
        bA, bB = 4, 5
        b.pe_wait_bank(bA)
        b.pe_wait_bank(bB)
        nc.tensor.wait_ge(b.sVE.h, rows_done)
        nc.tensor.matmul(b.PB[bA][:], one1_sb[:], a_ln[0:1, ts:ts + TC], start=True, stop=True)
        ins = nc.tensor.matmul(b.PB[bB][:], one1_sb[:], bneg_ln[0:1, ts:ts + TC], start=True, stop=True)
        bc_done = b.sPE.tick(ins)
        ins = nc.vector.tensor_copy(ab_sb[:, 0, :], b.PB[bA][:])
        ins._wait_ge(b.sPE.h, bc_done)
        b.evac_marks_bank(ins, bA, b.sVE)
        ins = nc.vector.tensor_copy(ab_sb[:, 1, :], b.PB[bB][:])
        b.evac_marks_bank(ins, bB, b.sVE)
        if sq_reader_pe:
            nc.vector.wait_ge(b.sPE.h, sq_reader_pe)
        for fc in range(NFC):
            nc.vector.tensor_mul(vtmp[:], x_fm[:, fc, ts:ts + TC], ab_sb[:, 0, :])
            ins = nc.vector.tensor_sub(dst_sl(fc), vtmp[:], ab_sb[:, 1, :])
        return b.sVE.tick(ins)

    def proj(wslot, rhs_src, w_rdy, rhs_sem, rhs_done, banks, evac, n_oc=NFC):
        nc.tensor.wait_ge(b.qW[wslot].h, b.qW[wslot].n)
        nc.tensor.wait_ge(rhs_sem.h, rhs_done)
        for oc in range(n_oc):
            bk = banks[oc % len(banks)]
            b.pe_wait_bank(bk)
            for kc in range(NFC):
                ins = nc.tensor.matmul(
                    b.PB[bk][:],
                    WS[wslot][:, kc, oc * 128:(oc + 1) * 128],
                    rhs_src(kc),
                    start=(kc == 0),
                    stop=(kc == 7),
                )
            mm_done = b.sPE.tick(ins)
            evac(oc, bk, mm_done)
        return b.sPE.n

    def ve_evac(dst):
        def f(oc, bk, mm_done):
            ins = nc.vector.tensor_copy(dst[:, oc, :], b.PB[bk][:])
            ins._wait_ge(b.sPE.h, mm_done)
            b.evac_marks_bank(ins, bk, b.sVE)
        return f

    for tc in range(NTC):
        rows_done, sq_pe = emit_ln_rows(tc)
        xn_done = emit_xn(tc, rows_done, sq_reader_pe=sq_pe)
        if tc == 0 and debug:
            b.barrier()
            dbg("a_ln", a_ln[:])
            dbg("xn0", xn[:])
        for j in range(4):
            tt = tc * 4 + j
            buf = j % 2
            nc.tensor.wait_ge(b.sVE.h, xn_done)
            if tt == 0:
                nc.tensor.wait_ge(b.qW[0].h, b.qW[0].n)
                nc.tensor.wait_ge(b.qW[1].h, b.qW[1].n)
            for half in range(2):
                b.pe_wait_bank(half)
                for kc in range(NFC):
                    ins = nc.tensor.matmul(
                        b.PB[half][:],
                        xn[:, kc, j * 128:(j + 1) * 128],
                        WS[0][:, kc, half * 512:(half + 1) * 512],
                        start=(kc == 0), stop=(kc == 7),
                    )
                b.sPE.tick(ins)
            for half in range(2):
                b.pe_wait_bank(2 + half)
                for kc in range(NFC):
                    ins = nc.tensor.matmul(
                        b.PB[2 + half][:],
                        xn[:, kc, j * 128:(j + 1) * 128],
                        WS[1][:, kc, half * 512:(half + 1) * 512],
                        start=(kc == 0), stop=(kc == 7),
                    )
                b.sPE.tick(ins)
            kvmm_done = b.sPE.n
            first = True
            for half in range(2):
                ins = nc.vector.tensor_copy(kt_bf[:, buf, half * 512:(half + 1) * 512], b.PB[half][:])
                if first:
                    ins._wait_ge(b.sPE.h, kvmm_done)
                    first = False
                b.evac_marks_bank(ins, half, b.sVE)
                ins = nc.vector.tensor_copy(vt_bf[:, buf, half * 512:(half + 1) * 512], b.PB[2 + half][:])
                b.evac_marks_bank(ins, 2 + half, b.sVE)
            ev_done = b.sVE.n
            b.pe_wait_bank(6)
            nc.tensor.wait_ge(b.sVE.h, ev_done)
            for h in range(NH):
                rb = (h % 2) * 64
                cb = (h // 2) * 64
                ins = nc.tensor.matmul(
                    b.PB[6][rb:rb + 64, cb:cb + 64],
                    kt_bf[:, buf, h * 64:(h + 1) * 64],
                    vt_bf[:, buf, h * 64:(h + 1) * 64],
                    start=True, stop=True,
                )
            kvp_done = b.sPE.tick(ins)
            if tt == 0:
                ins = nc.vector.tensor_copy(kv_acc[:], b.PB[6][:])
            else:
                ins = nc.vector.tensor_add(kv_acc[:], kv_acc[:], b.PB[6][:])
            ins._wait_ge(b.sPE.h, kvp_done)
            b.evac_marks_bank(ins, 6, b.sVE)
        b.barrier()
    ins = nc.vector.tensor_copy(kv_sb[:], kv_acc[:])
    b.sVE.tick(ins)
    b.barrier()
    dbg("kv", kv_acc[:])

    for tc in range(NTC):
        ts = tc * TC
        cast_load(0, wqT)
        cast_load(1, w1)
        xn_done = emit_xn(tc, b.sVE.n)
        xn_sl = lambda kc: xn[:, kc, :]
        q_done = proj(0, xn_sl, None, b.sVE, xn_done, (0, 1), ve_evac(q_bf))
        cast_load(0, wa1, wait_pe=q_done)
        nc.tensor.wait_ge(b.sVE.h, b.sVE.n)
        for hp in range(8):
            bk = 2 + hp % 2
            b.pe_wait_bank(bk)
            for sub in range(2):
                h = 2 * hp + sub
                rb = (h % 2) * 64
                ins = nc.tensor.matmul(
                    b.PB[bk][rb:rb + 64, :],
                    kv_sb[rb:rb + 64, (h // 2) * 64:(h // 2 + 1) * 64],
                    q_bf[rb:rb + 64, h // 2, :],
                    start=True, stop=True,
                )
            at_done = b.sPE.tick(ins)
            ins = nc.scalar.activation(at_bf[:, hp, :], b.PB[bk][:], AF.Copy, scale=0.125)
            ins._wait_ge(b.sPE.h, at_done)
            b.evac_marks_bank(ins, bk, b.sSE)
        at_ready = b.sSE.n
        h1_done = proj(1, xn_sl, None, b.sVE, xn_done, (0, 1), ve_evac(h1_bf))
        cast_load(1, wa2, wait_pe=h1_done)
        h2_done = proj(0, lambda kc: h1_bf[:, kc, :], None, b.sVE, b.sVE.n, (2, 3), ve_evac(h2_bf))
        cast_load(0, woT, wait_pe=h2_done)
        nc.tensor.wait_ge(b.qW[0].h, b.qW[0].n)
        nc.tensor.wait_ge(b.qW[1].h, b.qW[1].n)
        nc.tensor.wait_ge(b.sVE.h, b.sVE.n)
        nc.tensor.wait_ge(b.sSE.h, at_ready)
        for oc in range(NFC):
            bg = 4 + (oc % 2)
            bs = 6 + (oc % 2)
            b.pe_wait_bank(bg)
            for kc in range(NFC):
                ins = nc.tensor.matmul(
                    b.PB[bg][:], WS[1][:, kc, oc * 128:(oc + 1) * 128],
                    h2_bf[:, kc, :], start=(kc == 0), stop=(kc == 7),
                )
            g_mm = b.sPE.tick(ins)
            b.pe_wait_bank(bs)
            for kc in range(NFC):
                ins = nc.tensor.matmul(
                    b.PB[bs][:], WS[0][:, kc, oc * 128:(oc + 1) * 128],
                    at_bf[:, kc, :], start=(kc == 0), stop=(kc == 7),
                )
            s_mm = b.sPE.tick(ins)
            ins = nc.scalar.activation(g_oc[:], b.PB[bg][:], AF.Tanh, bias=fb_sb[:, oc:oc + 1])
            ins._wait_ge(b.sPE.h, g_mm)
            b.evac_marks_bank(ins, bg, b.sSE)
            ins = nc.scalar.activation(g_oc[:], g_oc[:], AF.Silu)
            silu_done = b.sSE.tick(ins)
            nc.vector.wait_ge(b.sSE.h, silu_done)
            ins = nc.vector.tensor_mul(vtmp[:], b.PB[bs][:], g_oc[:])
            ins._wait_ge(b.sPE.h, s_mm)
            b.evac_marks_bank(ins, bs, b.sVE)
            nc.vector.tensor_add(x_fm[:, oc, ts:ts + TC], x_fm[:, oc, ts:ts + TC], vtmp[:])
        b.barrier()
        if tc == 0:
            dbg("q0", q_bf[:])
            dbg("at0", at_bf[:])

    return dict(
        x_fm=x_fm, emit_ln_rows=emit_ln_rows, emit_xn=emit_xn, dbg=dbg,
        ones_sb=ones_sb, onesb_sb=onesb_sb, one1_sb=one1_sb, eps_sb=eps_sb,
        idn_sb=idn_sb, a_ln=a_ln, bneg_ln=bneg_ln, rowtmp=rowtmp, ab_sb=ab_sb,
        vtmp=vtmp,
    )



def l0_host_inputs(ins):
    c = np.ascontiguousarray
    return {
        "L0_wqT": c(ins["L0_attn_W_q"].T),
        "L0_wkT": c(ins["L0_attn_W_k"].T),
        "L0_wvT": c(ins["L0_attn_W_v"].T),
        "L0_woT": c(ins["L0_attn_W_o"].T),
        "L0_w1": c(ins["L0_ffn_weight"]),
        "L0_wa1": c(ins["L0_ffn_aux_weight1"]),
        "L0_wa2": c(ins["L0_ffn_aux_weight2"]),
        "L0_fbias": c(ins["L0_ffn_bias"].reshape(8, 128).T),
        "C_ones128": np.ones((128, 1), np.float32),
        "C_ones128b": np.ones((128, 1), BF),
        "C_one1": np.ones((1, 128), np.float32),
        "C_idn": np.eye(128, dtype=np.float32),
        "C_eps": np.ascontiguousarray(np.broadcast_to(np.array([[1e-5, 1e-6]], np.float32), (128, 2))),
    }


OFF_XN2 = 93184

def emit_l1_attn(b: Build, c, debug=None, s_d=None, x1_d=None, until='C'):
    nc = b.nc
    dbg = c["dbg"]
    x_fm = c["x_fm"]
    one1_sb = c["one1_sb"]
    ones_sb = c["ones_sb"]
    onesb_sb = c["onesb_sb"]
    eps_sb = c["eps_sb"]
    idn_sb = c["idn_sb"]

    wq = b.din("L1_wq", (D, 1040))
    wk = b.din("L1_wk", (D, D))
    wv = b.din("L1_wv", (D, D))
    wo = b.din("L1_wo", (D, D))
    bqq = b.din("L1_bqq", (128, 8))
    bqg = b.din("L1_bqg", (16, 1))
    bkk = b.din("L1_bkk", (128, 8))
    bvpad = b.din("L1_bvpad", (128, 16, 64), BF16)
    boc = b.din("L1_boc", (128, 8))
    eqn = b.din("L1_eqn", (2, 128), BF16)
    ekn = b.din("L1_ekn", (2, 128), BF16)
    mask2 = b.din("L1_mask2", (128, 2), BF16)
    one1b = b.din("C_one1b", (1, 128), BF16)
    ones65f = b.din("C_ones65f", (65, 64))
    ones65b = b.din("C_ones65b", (65, 64), BF16)
    ones2048b = b.din("C_ones2048b", (1, T), BF16)
    ones16b = b.din("C_ones16b", (128, 16), BF16)

    if x1_d is None:
        x1_d = b.dscratch("x1_d", (128, NFC, T))
    v_d = b.dscratch("v_d", (16, 128, 16, 65), BF16)
    mx_d = b.dscratch("mx_d", (256, 128))
    g_d = b.dscratch("g_d", (16, T))
    at_d = b.dscratch("at_d", (128, NFC, T), BF16)
    if s_d is None:
        s_d = b.dscratch("s_d", (128, NFC, T))

    xn2 = b.sba("xn2", [128, NFC, T], BF16, OFF_XN2)
    Qh = b.sba("Qh", [128, NFC, T], BF16, OFF_BIG)
    Kh = b.sba("Kh", [128, NFC, T], BF16, OFF_BIG + 32768)
    wsl_q = b.sba("wsl_q", [128, NFC, 1040], BF16, OFF_SLOT0)
    wsl_v = b.sba("wsl_v", [128, NFC, KB], BF16, OFF_SLOT0)
    wsl_1 = b.sba("wsl_1", [128, NFC, KB], BF16, OFF_SLOT1)
    sq2 = b.sba("sq2", [128, 2, TC], BF16, 131072)
    qpre = b.sba("qpre", [128, 2, TC], F32, 133120)
    bc2m = b.sba("bc2m", [128, TC], F32, 126976)
    vstage = b.sba("vstage", [128, 2, 16, 65], BF16, 131072)
    vslab = b.sba("vslab", [128, 2, 16, 65], BF16, 135232)
    PT = b.sba("PT", [128, 2048], BF16, 139392)
    mxneg = b.sba("mxneg", [128, 256], F32, 143488)
    atst = b.sba("atst", [64, TC], BF16, 144512)
    m4 = b.sba("m4", [128, 8], F32, 145536)
    sstage = b.sba("sstage", [128, 2, TC], F32, 126976)
    at_slab = b.sba("at_slab", [128, NFC, TC], BF16, 131072)
    G_sb = b.sba("G_sb", [16, T], F32, OFF_AROW)
    g64 = b.sba("g64", [65, T], F32, OFF_AROW)
    rs64 = b.sba("rs64", [65, TC], F32, OFF_BROW)
    rrow = b.sba("rrow", [2, 2, TC], F32, OFF_RTMP)
    rrowb = b.sba("rrowb", [2, TC], BF16, OFF_AB + 3072)
    mxT = b.sba("mxT", [128, 2, 128], F32, OFF_RTMP)
    Qx = b.sba("Qx", [65, T], BF16, 40960)
    Kx = b.sba("Kx", [65, T], BF16, 45056)
    ones65f_sb = b.sba("ones65f_sb", [65, 64], F32, 32768 + 2048)
    rs64b = b.sba("rs64b", [65, TC], BF16, 32768 + 2304)
    ones65b_sb = b.sba("ones65b_sb", [65, 64], BF16, 32768 + 3328)
    eqn_sb = b.sba("eqn_sb", [2, 128], BF16, OFF_AB)
    ekn_sb = b.sba("ekn_sb", [2, 128], BF16, OFF_AB + 512)
    one1b_sb = b.sba("one1b_sb", [1, 128], BF16, OFF_CONST + 1152)
    c["one1b_sb"] = one1b_sb
    ones16b_sb = b.sba("ones16b_sb", [128, 16], BF16, OFF_CONST + 1760)
    mask2_sb = b.sba("mask2_sb", [128, 2], BF16, OFF_CONST + 1408)
    bqq_sb = b.sba("bqq_sb", [128, 8], F32, OFF_CONST + 1440)
    bkk_sb = b.sba("bkk_sb", [128, 8], F32, OFF_CONST + 1472)
    boc_sb = b.sba("boc_sb", [128, 8], F32, OFF_CONST + 1504)
    bqg_sb = b.sba("bqg_sb", [16, 1], F32, OFF_CONST + 1536)
    bvp_sb = b.sba("bvp_sb", [128, 16, 64], BF16, OFF_AB + 1024)


    for tc in range(NTC):
        rows_done, sq_pe = c["emit_ln_rows"](tc)
        ts = tc * TC
        c["emit_xn"](tc, rows_done, sq_reader_pe=sq_pe,
                     dst_sl=lambda fc: xn2[:, fc, ts:ts + TC])
    b.barrier()
    dbg("xn2", xn2[:])
    spill = b.qS.tick(nc.sync.dma_start(out=x1_d, in_=x_fm[:]), 16)

    for src, dst in ((eqn, eqn_sb), (ekn, ekn_sb), (mask2, mask2_sb),
                     (bqq, bqq_sb), (bkk, bkk_sb), (boc, boc_sb),
                     (bqg, bqg_sb), (one1b, one1b_sb), (bvpad, bvp_sb),
                     (ones65f, ones65f_sb), (ones16b, ones16b_sb),
                     (ones65b, ones65b_sb)):
        b.qS.tick(nc.sync.dma_start(out=dst[:], in_=src), 16)
    aconst = b.qS.n
    b.qW[0].tick(nc.gpsimd.dma_start(
        out=wsl_q[:], in_=wq.rearrange("(k p) o -> p k o", p=128)), 16)
    b.qW[1].tick(nc.gpsimd.dma_start(
        out=wsl_1[:], in_=wk.rearrange("(k p) o -> p k o", p=128)), 16)

    def proj_rms(dst, wslot_ap, bcol, expander, spill_wait):
        pend = None

        def epilogue(st):
            oc, tc, bk, par, sq2_done = st
            ts = tc * TC
            b.pe_wait_bank(2)
            nc.tensor.wait_ge(b.sSE.h, sq2_done)
            ins = nc.tensor.matmul(b.PB[2][0:2, :], mask2_sb[:], sq2[:, par, :], start=True, stop=True)
            r_done = b.sPE.tick(ins)
            nc.scalar.wait_ge(b.sPE.h, r_done)
            ins = nc.scalar.activation(rrow[:, par, :], b.PB[2][0:2, :], AF.Sqrt,
                                       scale=1.0 / HD, bias=eps_sb[0:2, 1:2])
            b.bank_free[2] = [(b.sSE, b.sSE.tick(ins))]
            nc.vector.wait_ge(b.sSE.h, b.sSE.n)
            with nc.allow_low_precision(reason="bf16 broadcast operand (sanctioned bf16 matmul inputs)"):
                ins = nc.vector.reciprocal(rrowb[:], rrow[:, par, :])
            rr_done = b.sVE.tick(ins)
            b.pe_wait_bank(3)
            nc.tensor.wait_ge(b.sVE.h, rr_done)
            ins = nc.tensor.matmul(b.PB[3][:], expander[:], rrowb[:], start=True, stop=True)
            bc_done = b.sPE.tick(ins)
            nc.vector.wait_ge(b.sPE.h, bc_done)
            ins = nc.vector.tensor_mul(dst[:, oc, ts:ts + TC], qpre[:, par, :], b.PB[3][:])
            b.evac_marks_bank(ins, 3, b.sVE)

        for oc in range(NFC):
            for tc in range(NTC):
                it = oc * NTC + tc
                par = it % 2
                ts = tc * TC
                bk = it % 2
                b.pe_wait_bank(bk)
                for kc in range(NFC):
                    ins = nc.tensor.matmul(
                        b.PB[bk][:], wslot_ap[:, kc, oc * 128:(oc + 1) * 128],
                        xn2[:, kc, ts:ts + TC], start=(kc == 0), stop=(kc == 7),
                    )
                mm_done = b.sPE.tick(ins)
                nc.scalar.wait_ge(b.sPE.h, mm_done)
                ins = nc.scalar.activation(sq2[:, par, :], b.PB[bk][:], AF.Square, bias=bcol[:, oc:oc + 1])
                sq2_done = b.sSE.tick(ins)
                nc.vector.wait_ge(b.sSE.h, sq2_done)
                if spill_wait is not None:
                    nc.vector.wait_ge(b.qS.h, b.qS.n)
                    spill_wait = None
                ins = nc.vector.tensor_scalar_add(qpre[:, par, :], b.PB[bk][:], bcol[:, oc:oc + 1])
                qpre_done = b.sVE.tick(ins)
                b.bank_free[bk] = [(b.sVE, qpre_done)]
                if pend is not None:
                    epilogue(pend)
                pend = (oc, tc, bk, par, sq2_done)
        epilogue(pend)

    nc.tensor.wait_ge(b.qW[0].h, b.qW[0].n)
    nc.tensor.wait_ge(b.qS.h, b.qS.n)
    nc.tensor.wait_ge(b.qC.h, b.qC.n)
    proj_rms(Qh, wsl_q, bqq_sb, eqn_sb, spill)
    q_use = b.sPE.n
    if until == "Aq":
        b.barrier()
        return dict(xn2=xn2, x1_d=x1_d, s_d=s_d)
    for tc in range(NTC):
        ts = tc * TC
        b.pe_wait_bank(4)
        for kc in range(NFC):
            ins = nc.tensor.matmul(
                b.PB[4][0:16, :], wsl_q[:, kc, 1024:1040], xn2[:, kc, ts:ts + TC],
                start=(kc == 0), stop=(kc == 7),
            )
        gmm = b.sPE.tick(ins)
        nc.scalar.wait_ge(b.sPE.h, gmm)
        ins = nc.scalar.activation(G_sb[:, ts:ts + TC], b.PB[4][0:16, :], AF.Sigmoid,
                                   bias=bqg_sb[:])
        b.bank_free[4] = [(b.sSE, b.sSE.tick(ins))]
    g_use = b.sPE.n
    ins = nc.sync.dma_start(out=g_d, in_=G_sb[:])
    ins._wait_ge(b.sSE.h, b.sSE.n)
    b.qS.tick(ins, 16)
    if until == "Ag":
        nc.sync.wait_ge(b.qS.h, b.qS.n)
        b.barrier()
        return dict(xn2=xn2, x1_d=x1_d, s_d=s_d)
    nc.tensor.wait_ge(b.qW[1].h, b.qW[1].n)
    proj_rms(Kh, wsl_1, bkk_sb, ekn_sb, None)
    k_use = b.sPE.n
    if until == "Ak":
        b.barrier()
        return dict(xn2=xn2, x1_d=x1_d, s_d=s_d)
    b.qW[0].tick(nc.gpsimd.dma_start(
        out=wsl_v[:], in_=wv.rearrange("(k p) o -> p k o", p=128))._wait_ge(b.sPE.h, max(q_use, g_use)), 16)
    b.qW[1].tick(nc.gpsimd.dma_start(
        out=wsl_1[:], in_=wo.rearrange("(k p) o -> p k o", p=128))._wait_ge(b.sPE.h, k_use), 16)
    nc.tensor.wait_ge(b.qW[0].h, b.qW[0].n)
    vst_free = [0, 0]
    for tt in range(16):
        buf = tt % 2
        for half in range(2):
            bk = 5 + half
            b.pe_wait_bank(bk)
            for kc in range(NFC):
                ins = nc.tensor.matmul(
                    b.PB[bk][:], xn2[:, kc, tt * 128:(tt + 1) * 128],
                    wsl_v[:, kc, half * 512:(half + 1) * 512],
                    start=(kc == 0), stop=(kc == 7),
                )
            b.sPE.tick(ins)
        vmm = b.sPE.n
        if vst_free[buf]:
            nc.vector.wait_ge(b.qV[buf].h, b.qV[buf].n)
        for half in range(2):
            ins = nc.vector.scalar_tensor_tensor(
                vstage[:, buf, half * 8:(half + 1) * 8, 0:64],
                b.PB[5 + half].rearrange("p (a c) -> p a c", c=64),
                1.0,
                bvp_sb[:, half * 8:(half + 1) * 8, :],
                ALU.mult, ALU.add,
            )
            if half == 0:
                ins._wait_ge(b.sPE.h, vmm)
            b.evac_marks_bank(ins, 5 + half, b.sVE)
        ins = nc.vector.tensor_copy(
            vstage[:, buf, :, 64:65], ones16b_sb.rearrange("p (a o) -> p a o", o=1))
        b.sVE.tick(ins)
        vwr = b.sVE.n
        ins = nc.sync.dma_start(out=v_d[tt], in_=vstage[:, buf])
        ins._wait_ge(b.sVE.h, vwr)
        vst_free[buf] = b.qV[buf].tick(ins, 16)
    nc.sync.wait_ge(b.qV[0].h, b.qV[0].n)
    nc.sync.wait_ge(b.qV[1].h, b.qV[1].n)
    nc.sync.wait_ge(b.qS.h, b.qS.n)
    b.barrier()
    dbg("Qh", Qh[:])
    dbg("Kh", Kh[:])
    if until == "A":
        return dict(xn2=xn2, x1_d=x1_d, s_d=s_d)

    for h in range(NH):
        rb = (h % 2) * 64
        hc = h // 2
        for qc in range(16):
            bs = 4 * (qc % 2)
            for j in range(4):
                b.pe_wait_bank(bs + j)
                ins = nc.tensor.matmul(
                    b.PB[bs + j][:],
                    Qh[rb:rb + 64, hc, qc * 128:(qc + 1) * 128],
                    Kh[rb:rb + 64, hc, j * 512:(j + 1) * 512],
                    start=True, stop=True,
                )
            smm = b.sPE.tick(ins)
            nc.vector.wait_ge(b.sPE.h, smm)
            for j in range(4):
                ins = nc.vector.reduce_max(m4[:, j:j + 1], b.PB[bs + j][:], axis=AX)
                b.evac_marks_bank(ins, bs + j, b.sVE)
            nc.vector.wait_ge(b.sVE.h, b.sVE.n)
            ins = nc.vector.reduce_max(m4[:, 4:5], m4[:, 0:4], axis=AX)
            nc.vector.wait_ge(b.sVE.h, b.sVE.tick(ins))
            ins = nc.vector.tensor_scalar_mul(mxneg[:, h * 16 + qc:h * 16 + qc + 1], m4[:, 4:5], -1.0)
            b.sVE.tick(ins)
    b.barrier()
    if until == "B1":
        return dict(xn2=xn2, x1_d=x1_d, s_d=s_d)
    for blk in range(2):
        ins = nc.tensor.transpose(b.PB[blk][:, 0:128], mxneg[:, blk * 128:(blk + 1) * 128], idn_sb[:])
        t_done = b.sPE.tick(ins)
        ins = nc.vector.tensor_copy(mxT[:, blk, :], b.PB[blk][:, 0:128])
        ins._wait_ge(b.sPE.h, t_done)
        b.evac_marks_bank(ins, blk, b.sVE)
    ins = nc.sync.dma_start(out=mx_d.rearrange("(a p) q -> p a q", p=128), in_=mxT[:])
    ins._wait_ge(b.sVE.h, b.sVE.n)
    b.qS.tick(ins, 16)
    nc.sync.wait_ge(b.qS.h, b.qS.n)
    b.barrier()
    if until == "B2":
        return dict(xn2=xn2, x1_d=x1_d, s_d=s_d)

    at_store = [0]
    ins = nc.sync.dma_start(out=Kx[64:65, :], in_=ones2048b)
    b.qS.tick(ins, 16)
    for h in range(NH):
        rb = (h % 2) * 64
        hcc = h // 2
        b.qS.tick(nc.sync.dma_start(out=Qx[0:64, :], in_=Qh[rb:rb + 64, hcc, :]), 16)
        b.qS.tick(nc.sync.dma_start(out=Kx[0:64, :], in_=Kh[rb:rb + 64, hcc, :]), 16)
        b.qG.tick(nc.gpsimd.dma_start(
            out=Qx[64:65, :], in_=mx_d[h * 16:(h + 1) * 16, :].rearrange("a q -> (a q)")[None, :]), 16)
        b.qS.tick(nc.sync.dma_start(out=g64[64:65, :], in_=g_d[h:h + 1, :]), 16)
        b.qS.tick(nc.sync.dma_start(
            out=vslab[:, 0], in_=v_d[:, :, h, :].rearrange("t p c -> p t c")), 16)
        nc.tensor.wait_ge(b.qG.h, b.qG.n)
        nc.tensor.wait_ge(b.qS.h, b.qS.n)
        nc.vector.wait_ge(b.qS.h, b.qS.n)
        def emit_evac(qgp):
            for u in range(2):
                qg = 2 * qgp + u
                nc.vector.wait_ge(b.sPE.h, acc_done[qgp])
                ins = nc.vector.reciprocal(rs64[64:65, :], b.PB[4 + u][64:65, :])
                nc.vector.wait_ge(b.sVE.h, b.sVE.tick(ins))
                with nc.allow_low_precision(reason="bf16 broadcast operand (sanctioned bf16 matmul inputs)"):
                    ins = nc.vector.tensor_mul(rs64b[64:65, :], rs64[64:65, :],
                                               g64[64:65, qg * 512:(qg + 1) * 512])
                b.sVE.tick(ins)
                b.pe_wait_bank(6)
                nc.tensor.wait_ge(b.sVE.h, b.sVE.n)
                ins = nc.tensor.matmul(b.PB[6][0:64, :], ones65b_sb[64:65, 0:64],
                                       rs64b[64:65, :], start=True, stop=True)
                bc_done = b.sPE.tick(ins)
                nc.vector.wait_ge(b.sPE.h, bc_done)
                ins = nc.vector.tensor_copy(bc2m[0:64, :], b.PB[6][0:64, :])
                b.evac_marks_bank(ins, 6, b.sVE)
                if at_store[0]:
                    nc.vector.wait_ge(b.qA.h, at_store[0])
                ins = nc.vector.tensor_mul(atst[:], b.PB[4 + u][0:64, :], bc2m[0:64, :])
                b.evac_marks_bank(ins, 4 + u, b.sVE)
                at_wr = b.sVE.n
                ins = nc.sync.dma_start(
                    out=at_d[rb:rb + 64, hcc, qg * 512:(qg + 1) * 512], in_=atst[:]
                )
                ins._wait_ge(b.sVE.h, at_wr)
                at_store[0] = b.qA.tick(ins, 16)

        acc_done = {}
        for qgp in range(2):
            exp_tick = {}
            out_tick = {}
            for kc in range(18):
                if kc < 16:
                    par = kc % 2
                    for u in range(2):
                        qg = 2 * qgp + u
                        b.pe_wait_bank(2 * par + u)
                        ins = nc.tensor.matmul(
                            b.PB[2 * par + u][:],
                            Kx[:, kc * 128:(kc + 1) * 128],
                            Qx[:, qg * 512:(qg + 1) * 512],
                            start=True, stop=True,
                        )
                    smm = b.sPE.tick(ins)
                    nc.scalar.wait_ge(b.sPE.h, max(smm, out_tick.get(kc - 2, 0)))
                    for u in range(2):
                        ins = nc.scalar.activation(
                            PT[:, par * 1024 + u * 512:par * 1024 + (u + 1) * 512],
                            b.PB[2 * par + u][:], AF.Exp, scale=0.125,
                        )
                        b.evac_marks_bank(ins, 2 * par + u, b.sSE)
                    exp_tick[kc] = b.sSE.n
                if kc == 2 and qgp == 1:
                    emit_evac(0)
                if kc >= 2:
                    ko = kc - 2
                    po = ko % 2
                    nc.tensor.wait_ge(b.sSE.h, exp_tick[ko])
                    if ko == 0:
                        b.pe_wait_bank(4)
                        b.pe_wait_bank(5)
                    for u in range(2):
                        ins = nc.tensor.matmul(
                            b.PB[4 + u][0:65, :], vslab[:, 0, ko, :],
                            PT[:, po * 1024 + u * 512:po * 1024 + (u + 1) * 512],
                            start=(ko == 0), stop=(ko == 15),
                        )
                    out_tick[ko] = b.sPE.tick(ins)
            acc_done[qgp] = b.sPE.n
        emit_evac(1)
        b.barrier()

    if until == "B3":
        return dict(xn2=xn2, x1_d=x1_d, s_d=s_d)
    nc.sync.wait_ge(b.qA.h, b.qA.n)
    nc.tensor.wait_ge(b.qW[1].h, b.qW[1].n)
    st_free = [0, 0]
    for tq in range(NTC):
        ts = tq * TC
        asl = b.qS.tick(nc.sync.dma_start(out=at_slab[:], in_=at_d[:, :, ts:ts + TC]), 16)
        nc.tensor.wait_ge(b.qS.h, b.qS.n)
        for oc in range(NFC):
            bk = oc % 2
            sbuf = oc % 2
            b.pe_wait_bank(bk)
            for kc in range(NFC):
                ins = nc.tensor.matmul(
                    b.PB[bk][:], wsl_1[:, kc, oc * 128:(oc + 1) * 128],
                    at_slab[:, kc, :], start=(kc == 0), stop=(kc == 7),
                )
            mm_done = b.sPE.tick(ins)
            if st_free[sbuf]:
                nc.vector.wait_ge(b.qT[sbuf].h, b.qT[sbuf].n)
            nc.vector.wait_ge(b.sPE.h, mm_done)
            ins = nc.vector.tensor_scalar_add(sstage[:, sbuf, :], b.PB[bk][:], boc_sb[:, oc:oc + 1])
            b.evac_marks_bank(ins, bk, b.sVE)
            swr = b.sVE.n
            ins = nc.sync.dma_start(out=s_d[:, oc, ts:ts + TC], in_=sstage[:, sbuf, :])
            ins._wait_ge(b.sVE.h, swr)
            st_free[sbuf] = b.qT[sbuf].tick(ins, 16)
        b.barrier()

    return dict(xn2=xn2, x1_d=x1_d, s_d=s_d)


def l1_host_inputs(ins):
    c = np.ascontiguousarray
    qn = np.asarray(ins["L1_attn_qn_w"], np.float32)
    kn = np.asarray(ins["L1_attn_kn_w"], np.float32)
    eqn = np.zeros((2, 128), np.float32)
    ekn = np.zeros((2, 128), np.float32)
    for j in range(2):
        eqn[j, j * 64:(j + 1) * 64] = qn
        ekn[j, j * 64:(j + 1) * 64] = kn
    mask2 = np.zeros((128, 2), np.float32)
    mask2[0:64, 0] = 1.0
    mask2[64:128, 1] = 1.0
    bv = np.asarray(ins["L1_attn_bv"], np.float32)
    bvpad = np.broadcast_to(bv.reshape(16, 64), (128, 16, 64))
    return {
        "L1_wq": c(ins["L1_attn_Wq"]),
        "L1_wk": c(ins["L1_attn_Wk"]),
        "L1_wv": c(ins["L1_attn_Wv"]),
        "L1_wo": c(ins["L1_attn_Wo"]),
        "L1_bqq": c(np.asarray(ins["L1_attn_bq"], np.float32)[:1024].reshape(8, 128).T),
        "L1_bqg": c(np.asarray(ins["L1_attn_bq"], np.float32)[1024:].reshape(16, 1)),
        "L1_bkk": c(np.asarray(ins["L1_attn_bk"], np.float32).reshape(8, 128).T),
        "L1_bvpad": c(bvpad.astype(BF)),
        "L1_boc": c(np.asarray(ins["L1_attn_bo"], np.float32).reshape(8, 128).T),
        "L1_eqn": eqn.astype(BF),
        "L1_ekn": ekn.astype(BF),
        "L1_mask2": mask2.astype(BF),
        "C_one1b": np.ones((1, 128), BF),
        "C_ones65f": np.concatenate([np.zeros((64, 64), np.float32), np.ones((1, 64), np.float32)], 0),
        "C_ones2048b": np.ones((1, 2048), BF),
        "C_ones16b": np.ones((128, 16), BF),
        "C_ones65b": np.concatenate([np.zeros((64, 64), np.float32), np.ones((1, 64), np.float32)], 0).astype(BF),
    }


MOE_TP = 4
MOE_TPW = 512

def emit_moe(b: Build, c, xn2, debug=None, moe_d=None):
    nc = b.nc
    dbg = c["dbg"]
    one1_sb = c["one1_sb"]
    idn_sb = c["idn_sb"]

    wg = b.din("L1_wg", (D, 12))
    wup = b.din("L1_wup", (12, 32, D, 128))
    wdn = b.din("L1_wdn", (12, 4096, D))
    bup = b.din("L1_bup", (128, 12, 32))
    bdn = b.din("L1_bdn", (12, D))

    if moe_d is None:
        moe_d = b.dscratch("moe_d", (128, NFC, T))
    cw_d = b.dscratch("cw_d", (12, T))

    TPW = 1024
    wdn_bf = b.sba("wdn_bf", [128, 32, KB], BF16, OFF_BIG)
    hA = b.sba("hA", [128, 16, TPW], BF16, OFF_SLOT0)
    hB = b.sba("hB", [128, 8, TPW], BF16, 24576)
    hC = b.sba("hC", [128, 3, TPW], BF16, 40960)
    hD = b.sba("hD", [128, 4, TPW], BF16, 84992)
    hE = b.sba("hE", [128, 1, TPW], BF16, 144384)

    def h_sl(hc, th=None):
        for t_, base, n_ in ((hA, 0, 16), (hB, 16, 8), (hC, 24, 3), (hD, 27, 4), (hE, 31, 1)):
            if hc < base + n_:
                if th is None:
                    return t_[:, hc - base, :]
                return t_[:, hc - base, th * 512:(th + 1) * 512]
        raise AssertionError

    wup_pan = b.sba("wup_pan", [128, 2, 2, NFC, 128], BF16, 125952)
    cwb = b.sba("cwb", [128, 512], F32, 134144)
    mtmp = b.sba("mtmp", [128, 2, 512], F32, 136192)
    cwrow = b.sba("cwrow", [1, T], BF16, 140288)
    bup_sb = b.sba("bup_sb", [128, 12, 32], F32, 47104)
    mtop = b.sba("mtop", [128, 6, 12], F32, 48640)
    mcat8 = b.sba("mcat8", [128, 8], F32, 48928)
    cw_tok = b.sba("cw_tok", [128, 12], F32, 48960)
    wg_sb = b.sba("wg_sb", [128, NFC, 12], BF16, OFF_CONST + 1568)
    cwT_sb = b.sba("cwT_sb", [12, T], F32, OFF_BIG)
    bdn_sb = b.sba("bdn_sb", [12, KB], F32, OFF_BIG + 8192)

    qWD = S(nc, "qWD")
    qU = [S(nc, "qU0"), S(nc, "qU1")]
    qM = [S(nc, "qM0"), S(nc, "qM1")]
    qF = S(nc, "qF")

    for src, dst in ((bup, bup_sb), (bdn, bdn_sb)):
        qF.tick(nc.sync.dma_start(out=dst[:], in_=src), 16)
    b.qG.tick(nc.gpsimd.dma_start(out=wg_sb[:], in_=wg.rearrange("(k p) o -> p k o", p=128)), 16)
    nc.tensor.wait_ge(qF.h, qF.n)
    nc.tensor.wait_ge(b.qG.h, b.qG.n)

    for tg in range(16):
        b.pe_wait_bank(0)
        for kc in range(NFC):
            ins = nc.tensor.matmul(
                b.PB[0][:, 0:12], xn2[:, kc, tg * 128:(tg + 1) * 128], wg_sb[:, kc, :],
                start=(kc == 0), stop=(kc == 7),
            )
        gmm = b.sPE.tick(ins)

        def vs(ins):
            v = b.sVE.tick(ins)
            nc.vector.wait_ge(b.sVE.h, v)
            return v
        nc.vector.wait_ge(b.sPE.h, gmm)
        ins = nc.vector.tensor_copy(mtop[:, 0, :], b.PB[0][:, 0:12])
        b.evac_marks_bank(ins, 0, b.sVE)
        nc.vector.wait_ge(b.sVE.h, b.sVE.n)
        L = mtop[:, 0, :]
        vs(nc.vector.reduce_max(mcat8[:, 0:1], L, axis=AX))
        vs(nc.vector.tensor_single_scalar(mtop[:, 1, :], L, mcat8[:, 0:1], ALU.is_equal))
        vs(nc.vector.scalar_tensor_tensor(mtop[:, 4, :], mtop[:, 1, :], -1e30, L, ALU.mult, ALU.add))
        vs(nc.vector.reduce_max(mcat8[:, 1:2], mtop[:, 4, :], axis=AX))
        vs(nc.vector.tensor_single_scalar(mtop[:, 2, :], mtop[:, 4, :], mcat8[:, 1:2], ALU.is_equal))
        vs(nc.vector.scalar_tensor_tensor(mtop[:, 5, :], mtop[:, 2, :], -1e30, mtop[:, 4, :], ALU.mult, ALU.add))
        vs(nc.vector.reduce_max(mcat8[:, 2:3], mtop[:, 5, :], axis=AX))
        vs(nc.vector.tensor_single_scalar(mtop[:, 3, :], mtop[:, 5, :], mcat8[:, 2:3], ALU.is_equal))
        ve1 = vs(nc.vector.tensor_scalar_mul(mcat8[:, 3:4], mcat8[:, 0:1], -1.0))
        nc.scalar.wait_ge(b.sVE.h, ve1)
        ins = nc.scalar.activation(mcat8[:, 4:7], mcat8[:, 0:3], AF.Exp, bias=mcat8[:, 3:4])
        se1 = b.sSE.tick(ins)
        nc.vector.wait_ge(b.sSE.h, se1)
        vs(nc.vector.reduce_sum(mcat8[:, 7:8], mcat8[:, 4:7], axis=AX))
        vs(nc.vector.reciprocal(mcat8[:, 7:8], mcat8[:, 7:8]))
        vs(nc.vector.tensor_scalar(cw_tok[:], mtop[:, 1, :], mcat8[:, 4:5], mcat8[:, 7:8], ALU.mult, ALU.mult))
        vs(nc.vector.tensor_scalar(mtop[:, 4, :], mtop[:, 2, :], mcat8[:, 5:6], mcat8[:, 7:8], ALU.mult, ALU.mult))
        vs(nc.vector.tensor_add(cw_tok[:], cw_tok[:], mtop[:, 4, :]))
        vs(nc.vector.tensor_scalar(mtop[:, 5, :], mtop[:, 3, :], mcat8[:, 6:7], mcat8[:, 7:8], ALU.mult, ALU.mult))
        ins = nc.vector.tensor_add(cw_tok[:], cw_tok[:], mtop[:, 5, :])
        cw_done = b.sVE.tick(ins)
        nc.vector.wait_ge(b.sVE.h, cw_done)
        b.pe_wait_bank(1)
        nc.tensor.wait_ge(b.sVE.h, cw_done)
        ins = nc.tensor.transpose(b.PB[1][0:12, 0:128], cw_tok[:], idn_sb[:])
        t_done = b.sPE.tick(ins)
        nc.vector.wait_ge(b.sPE.h, t_done)
        ins = nc.vector.tensor_copy(cwT_sb[:, tg * 128:(tg + 1) * 128], b.PB[1][0:12, 0:128])
        b.evac_marks_bank(ins, 1, b.sVE)
    ins = nc.sync.dma_start(out=cw_d, in_=cwT_sb[:])
    ins._wait_ge(b.sVE.h, b.sVE.n)
    qF.tick(ins, 16)
    nc.sync.wait_ge(qF.h, qF.n)
    b.barrier()
    dbg("cw", cwT_sb[:])

    for oc in range(NFC):
        for tq in range(NTC):
            bk = tq % 2
            buf = tq % 2
            b.pe_wait_bank(bk)
            ins = nc.tensor.matmul(
                b.PB[bk][:], bdn_sb[:, oc * 128:(oc + 1) * 128],
                cwT_sb[:, tq * TC:(tq + 1) * TC], start=True, stop=True,
            )
            mm_done = b.sPE.tick(ins)
            nc.vector.wait_ge(qM[buf].h, qM[buf].n)
            nc.vector.wait_ge(b.sPE.h, mm_done)
            ins = nc.vector.tensor_copy(mtmp[:, buf, :], b.PB[bk][:])
            b.evac_marks_bank(ins, bk, b.sVE)
            ins = nc.gpsimd.dma_start(out=moe_d[:, oc, tq * TC:(tq + 1) * TC], in_=mtmp[:, buf, :])
            ins._wait_ge(b.sVE.h, b.sVE.n)
            qM[buf].tick(ins, 16)
    bdsum_pe = b.sPE.n

    phaseB_done = bdsum_pe
    cwb_done = 0
    pan_use = [0, 0]
    for e in range(12):
        ins = nc.gpsimd.dma_start(out=cwrow[:], in_=cw_d[e:e + 1, :])
        if cwb_done:
            ins._wait_ge(b.sPE.h, cwb_done)
        b.qG.tick(ins, 16)
        wdn_waited = False
        hoist_skip = 0
        for tp in range(2):
            t0 = tp * TPW
            gelu_guard = phaseB_done
            for hc2 in range(16):
                if tp == 0 and hc2 % 2 == 1:
                    j = hc2 // 2
                    ins = nc.gpsimd.dma_start(
                        out=wdn_bf[:, 4 * j:4 * j + 4, :],
                        in_=wdn[e, 512 * j:512 * (j + 1)].rearrange("(k p) o -> p k o", p=128))
                    if j == 0:
                        ins._wait_ge(b.sPE.h, phaseB_done)
                    qWD.tick(ins, 16)
                pbuf = hc2 % 2
                if hc2 >= hoist_skip:
                    ins = nc.gpsimd.dma_start(
                        out=wup_pan[:, pbuf],
                        in_=wup[e, 2 * hc2:2 * hc2 + 2].rearrange("a (k p) h -> p a k h", p=128))
                    if pan_use[pbuf]:
                        ins._wait_ge(b.sPE.h, pan_use[pbuf])
                    qU[pbuf].tick(ins, 16)
                nc.tensor.wait_ge(qU[pbuf].h, qU[pbuf].n)
                for sub in range(2):
                    hc = 2 * hc2 + sub
                    for th in range(2):
                        bk = th
                        b.pe_wait_bank(bk)
                        for kc in range(NFC):
                            ins = nc.tensor.matmul(
                                b.PB[bk][:], wup_pan[:, pbuf, sub, kc, :],
                                xn2[:, kc, t0 + th * 512:t0 + (th + 1) * 512],
                                start=(kc == 0), stop=(kc == 7),
                            )
                        amm = b.sPE.tick(ins)
                        nc.scalar.wait_ge(b.sPE.h, max(amm, gelu_guard))
                        gelu_guard = 0
                        ins = nc.scalar.activation(
                            h_sl(hc, th), b.PB[bk][:], AF.Gelu, bias=bup_sb[:, e, hc:hc + 1])
                        b.evac_marks_bank(ins, bk, b.sSE)
                pan_use[pbuf] = b.sPE.n
            a_done_se = b.sSE.n
            ne = e if tp == 0 else e + 1
            if ne < 12:
                for hc2p in range(2):
                    pbuf = hc2p % 2
                    ins = nc.gpsimd.dma_start(
                        out=wup_pan[:, pbuf],
                        in_=wup[ne, 2 * hc2p:2 * hc2p + 2].rearrange("a (k p) h -> p a k h", p=128))
                    if pan_use[pbuf]:
                        ins._wait_ge(b.sPE.h, pan_use[pbuf])
                    qU[pbuf].tick(ins, 16)
                hoist_skip = 2
            else:
                hoist_skip = 0
            if not wdn_waited:
                nc.tensor.wait_ge(qWD.h, qWD.n)
                nc.tensor.wait_ge(qF.h, qF.n)
                wdn_waited = True
            nc.tensor.wait_ge(b.sSE.h, a_done_se)
            for th in range(2):
                b.pe_wait_bank(4)
                ins = nc.tensor.matmul(b.PB[4][:], c["one1b_sb"],
                                       cwrow[0:1, t0 + th * 512:t0 + (th + 1) * 512],
                                       start=True, stop=True)
                cwb_done = b.sPE.tick(ins)
                nc.vector.wait_ge(b.sPE.h, cwb_done)
                ins = nc.vector.tensor_copy(cwb[:], b.PB[4][:])
                b.evac_marks_bank(ins, 4, b.sVE)
                cwb_rdy = b.sVE.n
                nc.tensor.wait_ge(b.sVE.h, cwb_rdy)
                for oc in range(NFC):
                    bk = 2 + oc % 2
                    buf = oc % 2
                    b.pe_wait_bank(bk)
                    for hc in range(32):
                        ins = nc.tensor.matmul(
                            b.PB[bk][:], wdn_bf[:, hc, oc * 128:(oc + 1) * 128], h_sl(hc, th),
                            start=(hc == 0), stop=(hc == 31),
                        )
                    bmm = b.sPE.tick(ins)
                    if oc % 2 == 0:
                        nc.vector.wait_ge(qM[0].h, qM[0].n)
                    nc.vector.wait_ge(b.sPE.h, bmm)
                    ins = nc.vector.tensor_mul(mtmp[:, buf, :], b.PB[bk][:], cwb[:])
                    b.evac_marks_bank(ins, bk, b.sVE)
                    if oc % 2 == 1:
                        ins = nc.gpsimd.dma_start(
                            out=moe_d[:, oc - 1:oc + 1, t0 + th * 512:t0 + (th + 1) * 512],
                            in_=mtmp.rearrange("p a c -> p a c"), accum_op=ALU.add)
                        ins._wait_ge(b.sVE.h, b.sVE.n)
                        qM[0].tick(ins, 16)
            phaseB_done = b.sPE.n
    nc.sync.wait_ge(qM[0].h, qM[0].n)
    nc.sync.wait_ge(qM[1].h, qM[1].n)
    b.barrier()
    return dict(moe_d=moe_d)




def emit_final(b: Build, c, x1_d, s_d, moe_d, y_out):
    nc = b.nc
    idn_sb = c["idn_sb"]
    x1s = b.sba("x1s", [128, NFC, TC], F32, 51200)
    ss = b.sba("ss", [128, NFC, TC], F32, 67584)
    ms = b.sba("ms", [128, NFC, TC], F32, 84992)
    ystage = b.sba("ystage", [128, 2, KB], F32, 101376)
    qY = S(nc, "qY")
    qL = S(nc, "qL")
    nc.sync.wait_ge(b.qT[0].h, b.qT[0].n)
    nc.sync.wait_ge(b.qT[1].h, b.qT[1].n)
    nc.sync.wait_ge(b.qS.h, b.qS.n)
    b.barrier()
    for tc in range(NTC):
        ts = tc * TC
        ins = nc.sync.dma_start(out=x1s[:], in_=x1_d[:, :, ts:ts + TC])
        if tc:
            ins._wait_ge(b.sVE.h, b.sVE.n)
        qL.tick(ins, 16)
        ins = nc.sync.dma_start(out=ss[:], in_=s_d[:, :, ts:ts + TC])
        if tc:
            ins._wait_ge(b.sVE.h, b.sVE.n)
        qL.tick(ins, 16)
        ins = nc.sync.dma_start(out=ms[:], in_=moe_d[:, :, ts:ts + TC])
        if tc:
            ins._wait_ge(b.sPE.h, b.sPE.n)
        ld = qL.tick(ins, 16)
        nc.scalar.wait_ge(qL.h, ld)
        half_done = []
        for hf in range(2):
            sl = slice(hf * 4, (hf + 1) * 4)
            ins = nc.scalar.activation(ms[:, sl, :], ms[:, sl, :], AF.Silu)
            se_done = b.sSE.tick(ins)
            nc.vector.wait_ge(b.sSE.h, se_done)
            nc.vector.tensor_mul(ms[:, sl, :], ms[:, sl, :], ss[:, sl, :])
            ins = nc.vector.tensor_add(ms[:, sl, :], ms[:, sl, :], x1s[:, sl, :])
            half_done.append(b.sVE.tick(ins))
        for ttt in range(4):
            ybuf = ttt % 2
            bks = (0, 1) if ttt % 2 == 0 else (2, 3)
            b.pe_wait_bank(bks[0])
            b.pe_wait_bank(bks[1])
            for fc in range(NFC):
                if fc == 0:
                    nc.tensor.wait_ge(b.sVE.h, half_done[0])
                if fc == 4:
                    nc.tensor.wait_ge(b.sVE.h, half_done[1])
                ins = nc.tensor.transpose(
                    b.PB[bks[fc // 4]][:, (fc % 4) * 128:(fc % 4 + 1) * 128],
                    ms[:, fc, ttt * 128:(ttt + 1) * 128],
                    idn_sb[:],
                )
            t_done = b.sPE.tick(ins)
            nc.vector.wait_ge(b.sPE.h, t_done)
            nc.vector.wait_ge(qY.h, qY.n)
            for j, bi in enumerate(bks):
                ins = nc.vector.tensor_copy(
                    ystage[:, ybuf, j * 512:(j + 1) * 512], b.PB[bi][:])
                b.evac_marks_bank(ins, bi, b.sVE)
            ywr = b.sVE.n
            row0 = ts + ttt * 128
            ins = nc.sync.dma_start(out=y_out[row0:row0 + 128, :], in_=ystage[:, ybuf, :])
            ins._wait_ge(b.sVE.h, ywr)
            qY.tick(ins, 16)
    nc.sync.wait_ge(qY.h, qY.n)
    return qY


def moe_host_inputs(ins):
    c = np.ascontiguousarray
    wup = np.asarray(ins["L1_ffn_Wup"], np.float32)
    bup = np.asarray(ins["L1_ffn_bup"], np.float32)
    return {
        "L1_wg": c(np.asarray(ins["L1_ffn_Wg"], np.float32)),
        "L1_wup": c(wup.reshape(12, 1024, 32, 128).transpose(0, 2, 1, 3)),
        "L1_wdn": c(np.asarray(ins["L1_ffn_Wdown"], np.float32)),
        "L1_bup": c(bup.reshape(12, 32, 128).transpose(2, 0, 1)),
        "L1_bdn": c(np.asarray(ins["L1_ffn_bdown"], np.float32)),
    }



def _host_inputs(ins):
    h = l0_host_inputs(ins)
    h.update(l1_host_inputs(ins))
    h.update(moe_host_inputs(ins))
    return h


def _build_full():
    b = Build()
    x_in = b.din("x_in", (2048, 1024))
    c = emit_l0(b, x_in)
    r1 = emit_l1_attn(b, c)
    r2 = emit_moe(b, c, r1["xn2"])
    y = b.dout("y", (2048, 1024))
    emit_final(b, c, r1["x1_d"], r1["s_d"], r2["moe_d"], y)
    return b


def kernel(**inputs):
    ins = {k: np.asarray(v) for k, v in inputs.items()}
    host = _host_inputs(ins)
    b = _build_full()
    x = np.asarray(ins["L0_attn_x"], np.float32)
    in_maps = []
    for core in range(8):
        m = dict(host)
        m["x_in"] = np.ascontiguousarray(x[core])
        in_maps.append(m)
    res = run_bass_kernel_spmd(b.nc, in_maps, core_ids=list(range(8)))
    out = np.stack([r["y"] for r in res.results], axis=0).astype(np.float32)
    return out
